# version 01
# baseline (speedup 1.0000x reference)
import numpy as np
import ml_dtypes
import concourse.bass as bass
import concourse.mybir as mybir
from concourse.bass_utils import run_bass_kernel_spmd

F32 = mybir.dt.float32
BF16 = mybir.dt.bfloat16
I32 = mybir.dt.int32
U8 = mybir.dt.uint8
U16 = mybir.dt.uint16
AF = mybir.ActivationFunctionType
ALU = mybir.AluOpType
AX = mybir.AxisListType
BF = ml_dtypes.bfloat16

T, D, NH, HD = 1024, 2048, 16, 128
NTC, KD = 8, 16
NBLK = 4
QKV_N, UP_N, INTER, DN_K = 6144, 8192, 4096, 32
GU_N, HFF, HFF_K = 16384, 8192, 64
F32EPS = float(np.finfo(np.float32).eps)
RSC = float(HD ** -0.5)
ARENA_KB = 204
KB = 1024
MAGIC = 1.5 * 2.0 ** 23

ENGINES = ("sync", "vector", "scalar", "tensor", "gpsimd")
DMA_Q = ("sync", "gpsimd")


class Plan:
    NPOOL = {"sync": 12, "gpsimd": 6}

    def __init__(self):
        self.q = {e: [] for e in ENGINES}
        self.semnames = [e for e in ENGINES if e not in DMA_Q]
        for qn, n in self.NPOOL.items():
            self.semnames += ["%s%d" % (qn, i) for i in range(n)]
        self.count = {sn: 0 for sn in self.semnames}
        self.dma_idx = {qn: 0 for qn in self.NPOOL}
        self.res = {}
        self.pending_fence = {e: [] for e in ENGINES}

    def fence(self):
        snap = [(sn, self.count[sn]) for sn in self.semnames if self.count[sn] > 0]
        for e in ENGINES:
            self.pending_fence[e] = list(snap)

    def _gather(self, eng, rd, wr, extra):
        deps = [t for t in extra if t is not None]
        if self.pending_fence[eng]:
            deps += self.pending_fence[eng]
            self.pending_fence[eng] = []
        for k in rd:
            e = self.res.get(k)
            if e and e[0]:
                deps.append(e[0])
        for k in wr:
            e = self.res.get(k)
            if e:
                if e[0]:
                    deps.append(e[0])
                deps.extend(e[1])
        return deps

    def _note(self, tok, rd, wr):
        for k in rd:
            self.res.setdefault(k, [None, []])[1].append(tok)
        for k in wr:
            self.res[k] = [tok, []]
        return tok

    def emit(self, eng, fn, rd=(), wr=(), extra=(), inc=1, tag="", nsw=False):
        assert eng not in DMA_Q
        deps = self._gather(eng, rd, wr, extra)
        if nsw:
            deps = [d for d in deps if d[0] != eng]
        self.q[eng].append((fn, deps, eng, 1, tag))
        self.count[eng] += 1
        return self._note((eng, self.count[eng]), rd, wr)

    def dma(self, queue, out_ap, in_ap, rd=(), wr=(), extra=(), tag=""):
        fn = (lambda o, i: lambda e: e.dma_start(out=o, in_=i))(out_ap, in_ap)
        sn = "%s%d" % (queue, self.dma_idx[queue] % self.NPOOL[queue])
        self.dma_idx[queue] += 1
        deps = self._gather(queue, rd, wr, extra)
        if self.count[sn] > 0:
            deps.append((sn, self.count[sn]))
        self.q[queue].append((fn, deps, sn, 16, tag))
        self.count[sn] += 16
        return self._note((sn, self.count[sn]), rd, wr)


def build_program(upto=99, tap_names=()):
    nc = bass.Bass("TRN2", target_bir_lowering=False)

    x_in = nc.dram_tensor("x", [T, D], F32, kind="ExternalInput").ap()
    wqkvT = nc.dram_tensor("wqkvT", [D, QKV_N], F32, kind="ExternalInput").ap()
    woT = nc.dram_tensor("woT", [D, D], F32, kind="ExternalInput").ap()
    wupT = nc.dram_tensor("wupT", [D, UP_N], F32, kind="ExternalInput").ap()
    wdnT = nc.dram_tensor("wdnT", [INTER, D], F32, kind="ExternalInput").ap()
    wguT = nc.dram_tensor("wguT", [D, GU_N], F32, kind="ExternalInput").ap()
    wprT = nc.dram_tensor("wprT", [HFF, D], F32, kind="ExternalInput").ap()
    anw_in = nc.dram_tensor("anw", [128, D], F32, kind="ExternalInput").ap()
    fnw_in = nc.dram_tensor("fnw", [128, D], F32, kind="ExternalInput").ap()
    cos_in = nc.dram_tensor("cos_t", [T, 64], F32, kind="ExternalInput").ap()
    sin_in = nc.dram_tensor("sin_t", [T, 64], F32, kind="ExternalInput").ap()
    mask0_in = nc.dram_tensor("mask0", [128, 128], F32, kind="ExternalInput").ap()
    mask1_in = nc.dram_tensor("mask1", [128, 256], F32, kind="ExternalInput").ap()
    idb_in = nc.dram_tensor("identb", [128, 128], BF16, kind="ExternalInput").ap()
    idf_in = nc.dram_tensor("identf", [128, 128], F32, kind="ExternalInput").ap()
    gsel_in = nc.dram_tensor("gsel", [128, 2], BF16, kind="ExternalInput").ap()
    gexp_in = nc.dram_tensor("gexp", [2, 128], BF16, kind="ExternalInput").ap()
    out_hbm = nc.dram_tensor("out", [T, D], F32, kind="ExternalOutput").ap()
    y_hbm = nc.dram_tensor("y_hbm", [T, D], F32, kind="Internal").ap()
    v_hbm = nc.dram_tensor("v_hbm", [T, D], BF16, kind="Internal").ap()
    x2_hbm = nc.dram_tensor("x2_hbm", [T, D], F32, kind="Internal").ap()
    xo1_hbm = nc.dram_tensor("xo1_hbm", [T, D], F32, kind="Internal").ap()
    wdnq_hbm = nc.dram_tensor("wdnq_hbm", [4, 8, 128, 2048], BF16, kind="Internal").ap()
    wprq_hbm = nc.dram_tensor("wprq_hbm", [4, 64, 128, 512], BF16, kind="Internal").ap()

    arena = nc.sbuf_tensor("arena", [128, ARENA_KB * KB], U8).__enter__()
    pm = [nc.psum_tensor("pm%d" % i, [128, 512], F32).__enter__() for i in range(5)]
    pq = nc.psum_tensor("pq", [128, 512], F32).__enter__()
    ptb_banks = [nc.psum_tensor("ptb%d" % i, [128, 1024], BF16).__enter__()
                 for i in range(2)]

    def V(off_b, rows, cols, dtype):
        sz = mybir.dt.size(dtype)
        o = int(off_b)
        assert o % sz == 0 and o + cols * sz <= ARENA_KB * KB, (o, cols, dtype)
        return arena[0:rows, o:o + cols * sz].bitcast(dtype)

    stats = V(0, 128, 448, F32)
    _sc = [0]
    def scol(n=1):
        c = _sc[0]; _sc[0] += n
        assert _sc[0] <= 448, "stats overflow"
        return stats[:, c:c + n]

    cosT = V(2 * KB, 128, 512, F32)
    sinT = V(4 * KB, 128, 512, F32)
    identb = V(6 * KB, 128, 128, BF16)
    identf = V(6400, 128, 128, F32)
    gsel = V(6912, 128, 2, BF16)
    gexp = V(6920, 2, 128, BF16)
    onesb = V(7180, 128, 1, BF16)
    onesf = V(7184, 128, 1, F32)
    mrow = V(7192, 1, 2, F32)
    brow = V(7208, 1, 128, F32)
    mask0 = V(8 * KB, 128, 128, F32)
    mask1 = V(9 * KB, 128, 256, F32)
    POOL = 10 * KB

    def PV(off_kb, rows, cols, dtype):
        return V(POOL + int(off_kb * KB), rows, cols, dtype)

    p = Plan()
    TAPS = {}

    def ts(out_ap, in_ap, s1, s2=None, op0=ALU.mult, op1=None, accum=None):
        def f(e):
            kw = {"op1": op1} if op1 is not None else {}
            return e.tensor_scalar(out=out_ap, in0=in_ap, scalar1=s1, scalar2=s2,
                                   op0=op0, accum_out=accum, **kw)
        return f

    def tt(o, a, b, op):
        return lambda e: e.tensor_tensor(out=o, in0=a, in1=b, op=op)

    def red(o, i, op, absval=False):
        return lambda e: e.tensor_reduce(out=o, in_=i, axis=AX.X, op=op,
                                         apply_absolute_value=absval)

    def cp(o, i):
        return lambda e: e.tensor_copy(o, i)

    def recip(o, i):
        return lambda e: e.reciprocal(o, i)

    def act(o, i, func, bias=0.0, scale=1.0, accum=None):
        return lambda e: e.activation(out=o, in_=i, func=func, bias=bias,
                                      scale=scale, accum_out=accum)

    def mmf(o, lhsT, rhs, start, stop):
        return lambda e: e.matmul(o, lhsT, rhs, start=start, stop=stop)

    def trf(o, i, ident):
        return lambda e: e.transpose(o, i, ident)

    _tb = [0]

    def transpose_group(items, dst_ap, dst_keys, extra_rd=()):
        assert len(items) <= 8
        b = _tb[0] % 2
        _tb[0] += 1
        pt = ptb_banks[b]
        for j, (sap, skeys) in enumerate(items):
            p.emit("tensor", trf(pt[:, j * 128:(j + 1) * 128], sap, identb),
                   rd=list(skeys) + [("const",)], wr=[("ptbank", b)],
                   nsw=True, tag="tr")
        return p.emit("vector", cp(dst_ap, pt[:, 0:len(items) * 128]),
                      rd=[("ptbank", b)] + list(extra_rd), wr=list(dst_keys))

    for dst, src in ((cosT, cos_in), (sinT, sin_in)):
        p.dma("sync", dst.rearrange("p (c f) -> p c f", c=8),
              src.rearrange("(c p) f -> p c f", p=128),
              wr=[("const",)], tag="c")
    for dst, src in ((identb, idb_in), (identf, idf_in), (gsel, gsel_in),
                     (gexp, gexp_in), (mask0, mask0_in), (mask1, mask1_in)):
        p.dma("sync", dst, src, wr=[("const",)], tag="c")
    p.emit("vector", lambda e: e.memset(onesb, 1.0), wr=[("const",)])
    p.emit("vector", lambda e: e.memset(onesf, 1.0), wr=[("const",)])
    cEPS5 = V(7200, 128, 1, F32)
    cEPSF = V(7204, 128, 1, F32)
    p.emit("vector", lambda e: e.memset(cEPS5, 1e-5), wr=[("const",)])
    p.emit("vector", lambda e: e.memset(cEPSF, F32EPS), wr=[("const",)])

    wfs = [PV(96, 128, 4096, F32), PV(112, 128, 4096, F32)]
    wqh = [PV(128, 128, 4096, BF16), PV(136, 128, 4096, BF16)]
    qa_s = PV(144, 128, 512, BF16)
    qb_s = PV(145, 128, 512, BF16)
    sinkb = PV(146, 128, 2048, BF16)
    msum = scol(96)
    inv_sw, sw_pp = {}, {}
    WSPEC = {"qkv": (wqkvT, D, QKV_N), "o": (woT, D, D), "dn": (wdnT, INTER, D)}
    coli = 0
    for name, (hbm, rows, cols) in WSPEC.items():
        flat = hbm.rearrange("(c p) f -> p c f", p=128)
        c0 = coli
        for r in range(rows // 128):
            for cc in range(0, cols, 2048):
                b = coli % 2
                p.dma("sync", wfs[b][:, 0:2048], flat[:, r, cc:cc + 2048],
                      wr=[("wfs", b)], tag="mean")
                p.emit("scalar", act(sinkb, wfs[b][:, 0:2048], AF.Abs,
                                     accum=msum[:, coli:coli + 1]),
                       rd=[("wfs", b)], wr=[("sink",), ("ms", coli)])
                coli += 1
        csum = scol()
        p.emit("vector", red(csum, msum[:, c0:coli], ALU.add),
               rd=[("ms", i) for i in range(c0, coli)], wr=[("cs", name)])
        p.emit("tensor", mmf(pq[0:1, 0:1], csum, onesf, True, True),
               rd=[("cs", name), ("const",)], wr=[("pq",)])
        p.emit("vector", ts(mrow[:, 0:1], pq[0:1, 0:1], 1.0 / (rows * cols), 1e-5,
                            ALU.mult, ALU.max), rd=[("pq",)], wr=[("mrow",)])
        p.emit("vector", cp(brow, mrow[:, 0:1].broadcast_to([1, 128])),
               rd=[("mrow",)], wr=[("brow",)])
        p.emit("tensor", trf(pq[:, 2:3], brow, identf[0:1, 0:1]),
               rd=[("brow",), ("const",), ("pq",)], wr=[("pq",)])
        isw = scol()
        p.emit("vector", cp(isw, pq[:, 2:3]), rd=[("pq",)], wr=[("isw", name)])
        swc = scol()
        p.emit("vector", recip(swc, isw), rd=[("isw", name)], wr=[("sw", name)])
        inv_sw[name] = isw
        sw_pp[name] = swc

    def qbit(name, wfs_ap, wqh_ap, bkey, k0, c0, nk):
        hbm, rows, cols = WSPEC[name]
        flat = hbm.rearrange("(c p) f -> p c f", p=128)
        p.dma("sync", wfs_ap.rearrange("p (k f) -> p k f", k=nk),
              flat[:, k0:k0 + nk, c0:c0 + 512], wr=[bkey], tag="w_" + name)
        for kk in range(nk):
            src = wfs_ap[:, kk * 512:(kk + 1) * 512]
            dst = wqh_ap[:, kk * 512:(kk + 1) * 512]
            p.emit("vector", ts(qa_s, src, sw_pp[name], 0.5, ALU.mult, ALU.is_gt),
                   rd=[bkey, ("sw", name)], wr=[("qa",)])
            p.emit("vector", ts(qb_s, src, sw_pp[name], -0.5, ALU.mult, ALU.is_lt),
                   rd=[bkey, ("sw", name)], wr=[("qb",)])
            p.emit("vector", tt(dst, qa_s, qb_s, ALU.subtract),
                   rd=[("qa",), ("qb",)], wr=[(bkey, kk)])

    TAPS["msum"] = (stats, 128)
    if upto <= 0:
        return _close(nc, p, TAPS, tap_names)

    p.fence()
    def hqT_t(k, tc):
        return PV((k * 8 + tc) * 0.25, 128, 128, BF16)
    t1s = PV(32, 128, D, F32)
    hqi = PV(40, 128, D, I32)
    hqb = PV(48, 128, D, BF16)
    sqs = PV(52, 128, D, BF16)
    anw_s = PV(56, 128, D, F32)
    y_s = PV(64, 128, D, F32)
    x_s = [PV(72, 128, D, F32), PV(80, 128, D, F32)]
    inv_sh = scol(8)

    def ln_stats(tc, x_ap, xkey, pref, sink_ap, sink_key):
        s1, ssq, m_, var_, al, nb = scol(), scol(), scol(), scol(), scol(), scol()
        p.emit("vector", red(s1, x_ap, ALU.add), rd=xkey, wr=[(pref + "s", tc)])
        p.emit("scalar", act(sink_ap, x_ap, AF.Square, accum=ssq),
               rd=xkey, wr=[(pref + "q", tc), sink_key])
        p.emit("vector", ts(m_, s1, 1.0 / D), rd=[(pref + "s", tc)],
               wr=[(pref + "m", tc)])
        p.emit("vector", ts(var_, m_, m_), rd=[(pref + "m", tc)],
               wr=[(pref + "v", tc)])
        p.emit("vector", ts(ssq, ssq, 1.0 / D), rd=[(pref + "q", tc)],
               wr=[(pref + "q", tc)])
        p.emit("vector", tt(var_, ssq, var_, ALU.subtract),
               rd=[(pref + "q", tc), (pref + "v", tc)], wr=[(pref + "v", tc)])
        p.emit("scalar", act(al, var_, AF.Ln, bias=cEPS5), rd=[(pref + "v", tc)],
               wr=[(pref + "a", tc)])
        p.emit("scalar", act(al, al, AF.Exp, scale=-0.5), rd=[(pref + "a", tc)],
               wr=[(pref + "a", tc)])
        p.emit("vector", ts(nb, m_, al, -1.0, ALU.mult, ALU.mult),
               rd=[(pref + "m", tc), (pref + "a", tc)], wr=[(pref + "n", tc)])
        return al, nb, var_, m_

    p.dma("sync", anw_s, anw_in, wr=[("anw",)], tag="anw")
    for tc in range(NTC):
        b = tc % 2
        p.dma("sync", x_s[b], x_in[tc * 128:(tc + 1) * 128, :], wr=[("xs", b)],
              tag="x")
        xkey = [("xs", b)]
        al1, nb1, var1, m1c = ln_stats(tc, x_s[b], xkey, "l1", sqs, ("sqs",))
        r2_, tmp = scol(), scol()
        p.emit("vector", ts(tmp, al1, al1), rd=[("l1a", tc)], wr=[("tm", tc)])
        p.emit("vector", tt(tmp, var1, tmp, ALU.mult),
               rd=[("l1v", tc), ("tm", tc)], wr=[("tm", tc)])
        p.emit("scalar", act(r2_, tmp, AF.Ln, bias=cEPS5), rd=[("tm", tc)],
               wr=[("r2", tc)])
        p.emit("scalar", act(r2_, r2_, AF.Exp, scale=-0.5), rd=[("r2", tc)],
               wr=[("r2", tc)])
        p.emit("vector", ts(y_s, x_s[b], m1c, None, ALU.subtract),
               rd=xkey + [("l1m", tc)], wr=[("ys",)])
        p.emit("vector", ts(y_s, y_s, al1), rd=[("ys",), ("l1a", tc)], wr=[("ys",)])
        p.dma("sync", y_hbm[tc * 128:(tc + 1) * 128, :], y_s, rd=[("ys",)],
              wr=[("yh", tc)], tag="y_out")
        p.emit("vector", tt(t1s, y_s, anw_s, ALU.mult), rd=[("ys",), ("anw",)],
               wr=[("t1",)])
        m1_, mh, c_ = scol(), scol(), scol()
        p.emit("vector", red(m1_, t1s, ALU.max, absval=True), rd=[("t1",)],
               wr=[("m1", tc)])
        p.emit("vector", ts(mh, m1_, r2_, 1e-5, ALU.mult, ALU.max),
               rd=[("m1", tc), ("r2", tc)], wr=[("mhc", tc)])
        p.emit("vector", ts(inv_sh[:, tc:tc + 1], mh, 1.0 / 127.0),
               rd=[("mhc", tc)], wr=[("ish", tc)])
        p.emit("vector", recip(c_, mh), rd=[("mhc", tc)], wr=[("chc", tc)])
        p.emit("vector", ts(c_, c_, r2_, 127.0, ALU.mult, ALU.mult),
               rd=[("r2", tc), ("chc", tc)], wr=[("chc", tc)])
        p.emit("vector", ts(t1s, t1s, c_, MAGIC, ALU.mult, ALU.add),
               rd=[("chc", tc), ("t1",)], wr=[("t1",)])
        p.emit("vector", ts(t1s, t1s, MAGIC, None, ALU.subtract),
               rd=[("t1",)], wr=[("t1",)])
        p.emit("vector", cp(hqi, t1s), rd=[("t1",)], wr=[("hqi",)])
        p.emit("scalar", act(hqb, hqi, AF.Copy), rd=[("hqi",)], wr=[("hqb",)])
        hqT_view = PV(0, 128, 16384, BF16).rearrange(
            "p (k t c) -> p k t c", k=16, t=8)
        for k0 in (0, 8):
            transpose_group(
                [(hqb[:, k * 128:(k + 1) * 128], [("hqb",)])
                 for k in range(k0, k0 + 8)],
                hqT_view[:, k0:k0 + 8, tc, :],
                [("hqT", k, tc) for k in range(k0, k0 + 8)])

    TAPS["hqT"] = (PV(0, 128, 16384, BF16), 128)
    TAPS["hqb"] = (hqb, 128)
    TAPS["stats"] = (stats, 128)
    if upto <= 1:
        return _close(nc, p, TAPS, tap_names)

    p.fence()
    def qT_t(h, tc):
        return PV(32 + (h * 8 + tc) * 0.25, 128, 128, BF16)
    def kT_t(h, tc, w=128):
        return PV(64 + (h * 8 + tc) * 0.25, 128, w, BF16)
    ropeA = PV(148, 128, 256, F32)
    ropeB = PV(149, 128, 256, F32)
    ropeQ = PV(150, 128, 512, BF16)
    vstage = PV(151, 128, 512, BF16)
    cos_sc = PV(152, 128, 64, F32)
    sin_sc = PV(152.25, 128, 64, F32)
    a_q = scol(8)
    for tc in range(NTC):
        p.emit("vector", ts(a_q[:, tc:tc + 1], inv_sh[:, tc:tc + 1], inv_sw["qkv"]),
               rd=[("ish", tc), ("isw", "qkv")], wr=[("aq", tc)])
    for w in range(12):
        qbit("qkv", wfs[0], wqh[0], ("wb", 0), 0, w * 512, 8)
        qbit("qkv", wfs[1], wqh[1], ("wb", 1), 8, w * 512, 8)
        for tc in range(NTC):
            bank = (w * NTC + tc) % 4
            for k in range(KD):
                b = k // 8
                p.emit("tensor", mmf(pm[bank][:], hqT_t(k, tc),
                                     wqh[b][:, (k % 8) * 512:(k % 8 + 1) * 512],
                                     k == 0, k == KD - 1),
                       rd=[("hqT", k, tc), (("wb", b), k % 8)], wr=[("pm", bank)],
                       tag="mm_qkv", nsw=(k != 0))
            if w < 8:
                is_q = w < 4
                h0 = (w % 4) * 4
                ps3 = pm[bank][:].rearrange("p (h f) -> p h f", h=4)
                p.emit("vector", ts(cos_sc, cosT[:, tc * 64:(tc + 1) * 64],
                                    a_q[:, tc:tc + 1]),
                       rd=[("aq", tc), ("const",)], wr=[("csc",)])
                p.emit("vector", ts(sin_sc, sinT[:, tc * 64:(tc + 1) * 64],
                                    a_q[:, tc:tc + 1]),
                       rd=[("aq", tc), ("const",)], wr=[("ssc",)])
                csb = cos_sc.unsqueeze(1).broadcast_to([128, 4, 64])
                snb = sin_sc.unsqueeze(1).broadcast_to([128, 4, 64])
                rA = ropeA.rearrange("p (h f) -> p h f", h=4)
                rB = ropeB.rearrange("p (h f) -> p h f", h=4)
                rQ = ropeQ.rearrange("p (h f) -> p h f", h=4)
                p.emit("vector", tt(rA, ps3[:, :, 0:64], csb, ALU.mult),
                       rd=[("pm", bank), ("csc",)], wr=[("rA",)])
                p.emit("vector", tt(rB, ps3[:, :, 64:128], snb, ALU.mult),
                       rd=[("pm", bank), ("ssc",)], wr=[("rB",)])
                p.emit("vector", tt(rQ[:, :, 0:64], rA, rB, ALU.subtract),
                       rd=[("rA",), ("rB",)], wr=[("rQ",)])
                p.emit("vector", tt(rA, ps3[:, :, 64:128], csb, ALU.mult),
                       rd=[("pm", bank), ("csc",), ("rQ",)], wr=[("rA",)])
                p.emit("vector", tt(rB, ps3[:, :, 0:64], snb, ALU.mult),
                       rd=[("pm", bank), ("ssc",), ("rQ",)], wr=[("rB",)])
                ev = p.emit("vector", tt(rQ[:, :, 64:128], rA, rB, ALU.add),
                            rd=[("rA",), ("rB",)], wr=[("rQ",)])
                base = 32 if is_q else 64
                dview = PV(base, 128, 16384, BF16).rearrange(
                    "p (h t c) -> p h t c", h=16, t=8)
                transpose_group(
                    [(ropeQ[:, hh * 128:(hh + 1) * 128], [("rQ",)])
                     for hh in range(4)],
                    dview[:, h0:h0 + 4, tc, :],
                    [("qT" if is_q else "kT", h0 + hh, tc) for hh in range(4)])
            else:
                vcol = (w - 8) * 512
                p.emit("vector", ts(vstage, pm[bank][:], a_q[:, tc:tc + 1]),
                       rd=[("pm", bank), ("aq", tc)], wr=[("vst",)])
                p.dma("sync", v_hbm[tc * 128:(tc + 1) * 128, vcol:vcol + 512],
                      vstage, rd=[("vst",)], wr=[("vh", tc)], tag="v_out")

    TAPS["qT"] = (PV(32, 128, 16384, BF16), 128)
    TAPS["kT"] = (PV(64, 128, 16384, BF16), 128)
    if upto <= 2:
        return _close(nc, p, TAPS, tap_names)

    p.fence()
    def ctxqT_t(k, tc):
        return PV((k * 8 + tc) * 0.25, 128, 128, BF16)
    v_rd = [PV(144, 128, D, BF16), PV(148, 128, D, BF16)]
    scrS = PV(152, 128, 384, F32)
    P0t = PV(154, 128, 128, BF16)
    P1t = PV(154.25, 128, 256, BF16)
    ctx_f = [PV(155, 128, D, F32), PV(163, 128, D, F32)]
    ctxq_i = PV(171, 128, D, I32)
    ctxq_b = PV(179, 128, D, BF16)
    inv_sctx = scol(8)
    rmaxc, rsumc, rrec, biasc = scol(2), scol(2), scol(2), scol(2)
    for b in range(NBLK):
        tc0, tc1 = 2 * b, 2 * b + 1
        for i, tcx in enumerate((tc0, tc1)):
            p.dma("sync", v_rd[i], v_hbm[tcx * 128:(tcx + 1) * 128, :],
                  rd=[("vh", tcx)], wr=[("vrd", i)], tag="v_rd")
        for h in range(NH):
            p.emit("tensor", mmf(pm[4][:, 0:128], qT_t(h, tc0), kT_t(h, tc0),
                                 True, True),
                   rd=[("qT", h, tc0), ("kT", h, tc0)], wr=[("pm", 4)], tag="mm_s")
            p.emit("tensor", mmf(pm[3][:, 0:256], qT_t(h, tc1),
                                 kT_t(h, tc0, 256), True, True),
                   rd=[("qT", h, tc1), ("kT", h, tc0), ("kT", h, tc1)],
                   wr=[("pm", 3)], tag="mm_s")
            p.emit("vector", tt(scrS[:, 0:128], pm[4][:, 0:128], mask0, ALU.add),
                   rd=[("pm", 4), ("const",)], wr=[("sS0",)])
            p.emit("vector", tt(scrS[:, 128:384], pm[3][:, 0:256], mask1,
                                ALU.add), rd=[("pm", 3), ("const",)], wr=[("sS1",)])
            p.emit("vector", red(rmaxc[:, 0:1], scrS[:, 0:128], ALU.max),
                   rd=[("sS0",)], wr=[("rm",)])
            p.emit("vector", red(rmaxc[:, 1:2], scrS[:, 128:384], ALU.max),
                   rd=[("sS1",)], wr=[("rm",)])
            p.emit("vector", ts(biasc, rmaxc, -RSC), rd=[("rm",)], wr=[("bia",)])
            p.emit("scalar", (lambda o_, i_, bi_, ac_: lambda e: e.activation(
                out=o_, in_=i_, func=AF.Exp, bias=bi_, scale=RSC, accum_out=ac_))(
                P0t, scrS[:, 0:128], biasc[:, 0:1], rsumc[:, 0:1]),
                rd=[("sS0",), ("bia",)], wr=[("P0",), ("rs",)])
            p.emit("scalar", (lambda o_, i_, bi_, ac_: lambda e: e.activation(
                out=o_, in_=i_, func=AF.Exp, bias=bi_, scale=RSC, accum_out=ac_))(
                P1t, scrS[:, 128:384], biasc[:, 1:2], rsumc[:, 1:2]),
                rd=[("sS1",), ("bia",)], wr=[("P1",), ("rs",)])
            p.emit("vector", recip(rrec, rsumc), rd=[("rs",)], wr=[("rr",)])
            transpose_group([(P0t, [("P0",)])], P0t, [("P0",)])
            transpose_group([(P1t[:, 0:128], [("P1",)]),
                             (P1t[:, 128:256], [("P1",)])], P1t, [("P1",)])
            p.emit("tensor", mmf(pm[0][:, 0:128], P0t,
                                 v_rd[0][:, h * 128:(h + 1) * 128], True, True),
                   rd=[("P0",), ("vrd", 0)], wr=[("pm", 0)], tag="mm_c")
            p.emit("tensor", mmf(pm[1][:, 0:128], P1t[:, 0:128],
                                 v_rd[0][:, h * 128:(h + 1) * 128], True, False),
                   rd=[("P1",), ("vrd", 0)], wr=[("pm", 1)], tag="mm_c")
            p.emit("tensor", mmf(pm[1][:, 0:128], P1t[:, 128:256],
                                 v_rd[1][:, h * 128:(h + 1) * 128], False, True),
                   rd=[("P1",), ("vrd", 1)], wr=[("pm", 1)], tag="mm_c", nsw=True)
            p.emit("vector", ts(ctx_f[0][:, h * 128:(h + 1) * 128], pm[0][:, 0:128],
                                rrec[:, 0:1]), rd=[("pm", 0), ("rr",)],
                   wr=[("cf0", h)])
            p.emit("vector", ts(ctx_f[1][:, h * 128:(h + 1) * 128],
                                pm[1][:, 0:128], rrec[:, 1:2]),
                   rd=[("pm", 1), ("rr",)], wr=[("cf1", h)])
        for qi, tcx in enumerate((tc0, tc1)):
            rkeys = [("cf%d" % qi, h) for h in range(NH)]
            mc, cc = scol(), scol()
            p.emit("vector", red(mc, ctx_f[qi], ALU.max, absval=True), rd=rkeys,
                   wr=[("mcx",)])
            p.emit("vector", ts(mc, mc, 1e-5, None, ALU.max), rd=[("mcx",)],
                   wr=[("mcx",)])
            p.emit("vector", ts(inv_sctx[:, tcx:tcx + 1], mc, 1.0 / 127.0),
                   rd=[("mcx",)], wr=[("isc", tcx)])
            p.emit("vector", recip(cc, mc), rd=[("mcx",)], wr=[("ccx",)])
            p.emit("vector", ts(cc, cc, 127.0), rd=[("ccx",)], wr=[("ccx",)])
            p.emit("vector", ts(ctx_f[qi], ctx_f[qi], cc, MAGIC, ALU.mult,
                                ALU.add), rd=[("ccx",)] + rkeys, wr=rkeys)
            p.emit("vector", ts(ctx_f[qi], ctx_f[qi], MAGIC, None, ALU.subtract),
                   rd=rkeys, wr=rkeys)
            p.emit("vector", cp(ctxq_i, ctx_f[qi]), rd=rkeys, wr=[("cqi",)])
            p.emit("scalar", act(ctxq_b, ctxq_i, AF.Copy), rd=[("cqi",)],
                   wr=[("cqb",)])
            cq_view = PV(0, 128, 16384, BF16).rearrange(
                "p (k t c) -> p k t c", k=16, t=8)
            for k0 in (0, 8):
                transpose_group(
                    [(ctxq_b[:, k * 128:(k + 1) * 128], [("cqb",)])
                     for k in range(k0, k0 + 8)],
                    cq_view[:, k0:k0 + 8, tcx, :],
                    [("cqT", k, tcx) for k in range(k0, k0 + 8)])

    TAPS["ctxqT"] = (PV(0, 128, 16384, BF16), 128)
    TAPS["ctxf0"] = (ctx_f[0], 128)
    TAPS["P0t"] = (P0t, 128)
    TAPS["P1t"] = (P1t, 128)
    TAPS["scrS"] = (scrS, 128)
    if upto <= 3:
        return _close(nc, p, TAPS, tap_names)

    p.fence()
    def y1_t(tc):
        return PV(32 + tc * 8, 128, D, F32)
    y_rd = [PV(146, 128, 512, F32), PV(148, 128, 512, F32)]
    oscr = PV(150, 128, 512, F32)
    a_o = scol(8)
    for tc in range(NTC):
        p.emit("vector", ts(a_o[:, tc:tc + 1], inv_sctx[:, tc:tc + 1],
                            inv_sw["o"]),
               rd=[("isc", tc), ("isw", "o")], wr=[("ao", tc)])
    for w in range(4):
        qbit("o", wfs[0], wqh[0], ("wb", 0), 0, w * 512, 8)
        qbit("o", wfs[1], wqh[1], ("wb", 1), 8, w * 512, 8)
        for tc in range(NTC):
            bank = (w * NTC + tc) % 4
            for k in range(KD):
                b = k // 8
                p.emit("tensor", mmf(pm[bank][:], ctxqT_t(k, tc),
                                     wqh[b][:, (k % 8) * 512:(k % 8 + 1) * 512],
                                     k == 0, k == KD - 1),
                       rd=[("cqT", k, tc), (("wb", b), k % 8)], wr=[("pm", bank)],
                       tag="mm_o", nsw=(k != 0))
            yb = tc % 2
            p.dma("sync", y_rd[yb],
                  y_hbm[tc * 128:(tc + 1) * 128, w * 512:(w + 1) * 512],
                  rd=[("yh", tc)], wr=[("yrd", yb)], tag="y_rd")
            p.emit("vector", ts(oscr, pm[bank][:], a_o[:, tc:tc + 1]),
                   rd=[("pm", bank), ("ao", tc)], wr=[("osc",)])
            p.emit("vector", tt(y1_t(tc)[:, w * 512:(w + 1) * 512], oscr,
                                y_rd[yb], ALU.add), rd=[("osc",), ("yrd", yb)],
                   wr=[("y1", tc, w)])

    TAPS["y1"] = (PV(32, 128, 16384, F32), 128)
    if upto <= 4:
        return _close(nc, p, TAPS, tap_names)

    p.fence()
    def hn1T_t(k, tc):
        return PV((k * 8 + tc) * 0.25, 128, 128, BF16)
    fnw_s = PV(152, 128, D, F32)
    hscr = PV(160, 128, D, F32)
    hb16 = PV(168, 128, D, BF16)
    sqs4 = PV(172, 128, D, BF16)
    xrd2 = [PV(176, 128, D, F32), PV(184, 128, D, F32)]
    p.dma("sync", fnw_s, fnw_in, wr=[("fnw",)], tag="fnw")
    for tc in range(NTC):
        y1k = [("y1", tc, w) for w in range(4)]
        ssq, rr2 = scol(), scol()
        p.emit("scalar", act(sqs4, y1_t(tc), AF.Square, accum=ssq),
               rd=y1k, wr=[("q2", tc), ("sqs4",)])
        p.emit("vector", ts(ssq, ssq, 1.0 / D), rd=[("q2", tc)], wr=[("q2", tc)])
        p.emit("scalar", act(rr2, ssq, AF.Ln, bias=cEPS5), rd=[("q2", tc)],
               wr=[("rr2", tc)])
        p.emit("scalar", act(rr2, rr2, AF.Exp, scale=-0.5), rd=[("rr2", tc)],
               wr=[("rr2", tc)])
        p.emit("vector", tt(hscr, y1_t(tc), fnw_s, ALU.mult), rd=y1k + [("fnw",)],
               wr=[("hscr",)])
        p.emit("vector", ts(hb16, hscr, rr2), rd=[("hscr",), ("rr2", tc)],
               wr=[("hb16",)])
        hn_view = PV(0, 128, 16384, BF16).rearrange(
            "p (k t c) -> p k t c", k=16, t=8)
        for k0 in (0, 8):
            transpose_group(
                [(hb16[:, k * 128:(k + 1) * 128], [("hb16",)])
                 for k in range(k0, k0 + 8)],
                hn_view[:, k0:k0 + 8, tc, :],
                [("hn1T", k, tc) for k in range(k0, k0 + 8)])
        xb = tc % 2
        p.dma("sync", xrd2[xb], x_in[tc * 128:(tc + 1) * 128, :], wr=[("xr2", xb)],
              tag="x_rd")
        p.emit("vector", tt(y1_t(tc), y1_t(tc), xrd2[xb], ALU.add),
               rd=y1k + [("xr2", xb)], wr=y1k)
        p.dma("sync", x2_hbm[tc * 128:(tc + 1) * 128, :], y1_t(tc), rd=y1k,
              wr=[("x2h", tc)], tag="x2_out")

    if upto <= 5:
        return _close(nc, p, TAPS, tap_names)

    p.fence()
    def sv_t(ti):
        return PV(32 + ti * 16, 128, INTER, F32)
    def svqT_t(k, tc):
        return PV(96 + (k * 8 + tc) * 0.25, 128, 128, BF16)
    upst = [PV(160, 128, 512, BF16), PV(161, 128, 512, BF16)]
    scrG4 = PV(162, 128, 2048, F32)
    svq_i = PV(170, 128, INTER, I32)
    svq_b = PV(186, 128, INTER, BF16)
    inv_ssv = scol(8)
    flatup = wupT.rearrange("(c p) f -> p c f", p=128)
    for g in range(2):
        tcs = list(range(g * 4, g * 4 + 4))
        for wp in range(8):
            for half, wofs in ((0, wp), (1, wp + 8)):
                for k in range(KD):
                    ub = k % 2
                    p.dma("gpsimd", upst[ub],
                          flatup[:, k, wofs * 512:(wofs + 1) * 512],
                          wr=[("upst", ub)], tag="w_up")
                    for ti, tc in enumerate(tcs):
                        p.emit("tensor", mmf(pm[ti][:], hn1T_t(k, tc), upst[ub],
                                             k == 0, k == KD - 1),
                               rd=[("hn1T", k, tc), ("upst", ub)],
                               wr=[("pm", ti)], tag="mm_up", nsw=(k != 0))
                if half == 0:
                    for ti, tc in enumerate(tcs):
                        p.emit("scalar", act(scrG4[:, ti * 512:(ti + 1) * 512],
                                             pm[ti][:], AF.Silu),
                               rd=[("pm", ti)], wr=[("sg", ti)])
                else:
                    for ti, tc in enumerate(tcs):
                        p.emit("vector", tt(sv_t(ti)[:, wp * 512:(wp + 1) * 512],
                                            scrG4[:, ti * 512:(ti + 1) * 512],
                                            pm[ti][:], ALU.mult),
                               rd=[("sg", ti), ("pm", ti)], wr=[("sv", ti, wp)])
        for ti, tc in enumerate(tcs):
            svk = [("sv", ti, wp) for wp in range(8)]
            mc, cc = scol(), scol()
            p.emit("vector", red(mc, sv_t(ti), ALU.max, absval=True), rd=svk,
                   wr=[("mcs",)])
            p.emit("vector", ts(mc, mc, 1e-5, None, ALU.max), rd=[("mcs",)],
                   wr=[("mcs",)])
            p.emit("vector", ts(inv_ssv[:, tc:tc + 1], mc, 1.0 / 127.0),
                   rd=[("mcs",)], wr=[("issv", tc)])
            p.emit("vector", recip(cc, mc), rd=[("mcs",)], wr=[("ccs",)])
            p.emit("vector", ts(cc, cc, 127.0), rd=[("ccs",)], wr=[("ccs",)])
            p.emit("vector", ts(sv_t(ti), sv_t(ti), cc, MAGIC, ALU.mult,
                                ALU.add), rd=[("ccs",)] + svk, wr=svk)
            p.emit("vector", ts(sv_t(ti), sv_t(ti), MAGIC, None, ALU.subtract),
                   rd=svk, wr=svk)
            p.emit("vector", cp(svq_i, sv_t(ti)), rd=svk, wr=[("svi",)])
            p.emit("scalar", act(svq_b, svq_i, AF.Copy), rd=[("svi",)],
                   wr=[("svb",)])
            sv_view = PV(96, 128, 32768, BF16).rearrange(
                "p (k t c) -> p k t c", k=32, t=8)
            for k0 in (0, 8, 16, 24):
                transpose_group(
                    [(svq_b[:, k * 128:(k + 1) * 128], [("svb",)])
                     for k in range(k0, k0 + 8)],
                    sv_view[:, k0:k0 + 8, tc, :],
                    [("svqT", k, tc) for k in range(k0, k0 + 8)])

    TAPS["svqT"] = (PV(96, 128, 32768, BF16), 128)
    if upto <= 6:
        return _close(nc, p, TAPS, tap_names)

    p.fence()
    def xo1_t(tc):
        return PV(32 + tc * 8, 128, D, F32)
    def zT_t(k, tc):
        return PV((k * 8 + tc) * 0.25, 128, 128, BF16)
    wfsd = [PV(160, 128, 2048, F32), PV(168, 128, 2048, F32)]
    wqhd = [PV(176, 128, 2048, BF16), PV(180, 128, 2048, BF16)]
    qa_d = PV(184, 128, 512, BF16)
    qb_d = PV(185, 128, 512, BF16)
    x2rd = [PV(186, 128, 512, F32), PV(188, 128, 512, F32)]
    oscr2 = PV(190, 128, 512, F32)
    a_dn = scol(8)
    for tc in range(NTC):
        p.emit("vector", ts(a_dn[:, tc:tc + 1], inv_ssv[:, tc:tc + 1],
                            inv_sw["dn"]),
               rd=[("issv", tc), ("isw", "dn")], wr=[("adn", tc)])

    flatdn = wdnT.rearrange("(c p) f -> p c f", p=128)
    for w in range(4):
        for g in range(2):
            tcs = list(range(g * 4, g * 4 + 4))
            for quarter in range(8):
                b = quarter % 2
                bkey = ("wbd", b)
                if g == 0:
                    p.dma("sync", wfsd[b].rearrange("p (k f) -> p k f", k=4),
                          flatdn[:, quarter * 4:quarter * 4 + 4,
                                 w * 512:(w + 1) * 512],
                          wr=[bkey], tag="w_dn")
                    for kk in range(4):
                        src = wfsd[b][:, kk * 512:(kk + 1) * 512]
                        dst = wqhd[b][:, kk * 512:(kk + 1) * 512]
                        p.emit("vector", ts(qa_d, src, sw_pp["dn"], 0.5, ALU.mult,
                                            ALU.is_gt), rd=[bkey, ("sw", "dn")],
                               wr=[("qa",)])
                        p.emit("vector", ts(qb_d, src, sw_pp["dn"], -0.5, ALU.mult,
                                            ALU.is_lt), rd=[bkey, ("sw", "dn")],
                               wr=[("qb",)])
                        p.emit("vector", tt(dst, qa_d, qb_d, ALU.subtract),
                               rd=[("qa",), ("qb",)], wr=[(bkey, kk), ("wqd", b)])
                    p.dma("sync", wdnq_hbm[w, quarter], wqhd[b],
                          rd=[(bkey, kk) for kk in range(4)] + [("wqd", b)],
                          wr=[("wdq", w, quarter)], tag="w_dnq_out")
                else:
                    p.dma("sync", wqhd[b], wdnq_hbm[w, quarter],
                          rd=[("wdq", w, quarter)],
                          wr=[(bkey, kk) for kk in range(4)] + [("wqd", b)],
                          tag="w_dnq_in")
                for ti, tc in enumerate(tcs):
                    bank = ti
                    for kk in range(4):
                        k = quarter * 4 + kk
                        p.emit("tensor", mmf(pm[bank][:], svqT_t(k, tc),
                                             wqhd[b][:, kk * 512:(kk + 1) * 512],
                                             k == 0, k == DN_K - 1),
                               rd=[("svqT", k, tc), (bkey, kk)],
                               wr=[("pm", bank)], tag="mm_dn", nsw=(k != 0))
                    if quarter == 7:
                        xb = tc % 2
                        p.dma("sync", x2rd[xb],
                              x2_hbm[tc * 128:(tc + 1) * 128,
                                     w * 512:(w + 1) * 512],
                              rd=[("x2h", tc)], wr=[("x2r", xb)], tag="x2_rd")
                        p.emit("vector", ts(oscr2, pm[bank][:],
                                            a_dn[:, tc:tc + 1]),
                               rd=[("pm", bank), ("adn", tc)], wr=[("osc2",)])
                        p.emit("vector", tt(xo1_t(tc)[:, w * 512:(w + 1) * 512],
                                            oscr2, x2rd[xb], ALU.add),
                               rd=[("osc2",), ("x2r", xb)], wr=[("xo1", tc, w)])
    p.fence()
    zb16 = PV(160, 128, D, BF16)
    sqs5 = PV(164, 128, D, BF16)
    zscr = PV(168, 128, D, F32)
    for tc in range(NTC):
        xok = [("xo1", tc, w) for w in range(4)]
        al3, nb3, _, m3c = ln_stats(tc, xo1_t(tc), xok, "l3", sqs5, ("sqs5",))
        p.emit("vector", ts(zscr, xo1_t(tc), m3c, None, ALU.subtract),
               rd=xok + [("l3m", tc)], wr=[("zsc",)])
        p.emit("vector", ts(zb16, zscr, al3), rd=[("zsc",), ("l3a", tc)],
               wr=[("zb",)])
        z_view = PV(0, 128, 16384, BF16).rearrange(
            "p (k t c) -> p k t c", k=16, t=8)
        for k0 in (0, 8):
            transpose_group(
                [(zb16[:, k * 128:(k + 1) * 128], [("zb",)])
                 for k in range(k0, k0 + 8)],
                z_view[:, k0:k0 + 8, tc, :],
                [("zT", k, tc) for k in range(k0, k0 + 8)])
        p.dma("sync", xo1_hbm[tc * 128:(tc + 1) * 128, :], xo1_t(tc), rd=xok,
              wr=[("xoh", tc)], tag="xo1_out")

    TAPS["xo1"] = (PV(32, 128, 16384, F32), 128)
    TAPS["zT"] = (PV(0, 128, 16384, BF16), 128)
    if upto <= 7:
        return _close(nc, p, TAPS, tap_names)

    p.fence()
    def h2_t(kh):
        if kh < 32:
            return PV(32 + kh * 2, 128, T, BF16)
        return PV(128 + (kh - 32) * 2, 128, T, BF16)
    wslab = PV(96, 128, 512, BF16)
    wabs = PV(97, 128, 512, BF16)
    shexp = PV(98, 128, 512, BF16)
    mind = PV(99, 128, 512, BF16)
    wsign = PV(100, 128, 512, U16)
    s_row = PV(101, 2, 512, U16)
    sh_row = PV(102, 2, 512, BF16)
    wq512 = [PV(103 + i, 128, 512, U16) for i in range(16)]
    scrGg = PV(119, 128, 4096, BF16)
    h2sq = PV(127, 128, 512, BF16)
    ssrow = PV(96, 1, T, F32)
    rr3r = PV(100, 1, T, F32)
    rr3c = scol(8)

    def ternary_slab(flat_ap, k, c0, dst_ap, dkey, width=512):
        p.dma("gpsimd", wslab[:, 0:width], flat_ap[:, k, c0:c0 + width],
              wr=[("wsl",)], tag="w_tern")
        p.emit("scalar", act(wabs[:, 0:width], wslab[:, 0:width], AF.Abs),
               rd=[("wsl",)], wr=[("wab",)])
        p.emit("tensor", mmf(pq[0:2, 0:width], gsel, wabs[:, 0:width], True, True),
               rd=[("wab",), ("const",)], wr=[("pq",)])
        p.emit("vector", ts(s_row.bitcast(BF16)[:, 0:width], pq[0:2, 0:width],
                            1.0 / 64.0, 1e-8, ALU.mult, ALU.max),
               rd=[("pq",)], wr=[("sro",)])
        p.emit("vector", ts(sh_row[:, 0:width], s_row.bitcast(BF16)[:, 0:width],
                            0.5), rd=[("sro",)], wr=[("shr",)])
        p.emit("tensor", mmf(pq[:, 0:width], gexp, sh_row[:, 0:width], True, True),
               rd=[("shr",), ("const",), ("pq",)], wr=[("pq",)])
        p.emit("vector", cp(shexp[:, 0:width], pq[:, 0:width]), rd=[("pq",)],
               wr=[("she",)])
        p.emit("vector", tt(mind[:, 0:width], wabs[:, 0:width], shexp[:, 0:width],
                            ALU.is_gt), rd=[("wab",), ("she",)], wr=[("mnd",)])
        p.emit("vector", tt(mind[:, 0:width], mind[:, 0:width],
                            shexp[:, 0:width], ALU.mult),
               rd=[("mnd",), ("she",)], wr=[("mnd",)])
        p.emit("vector", ts(mind[:, 0:width], mind[:, 0:width], 2.0),
               rd=[("mnd",)], wr=[("mnd",)])
        p.emit("vector", ts(wsign[:, 0:width], wslab.bitcast(U16)[:, 0:width],
                            0x8000, None, ALU.bitwise_and), rd=[("wsl",)],
               wr=[("wsg",)])
        return p.emit("vector", tt(dst_ap[:, 0:width],
                                   mind.bitcast(U16)[:, 0:width],
                                   wsign[:, 0:width], ALU.bitwise_or),
                      rd=[("mnd",), ("wsg",)], wr=[dkey])

    flatgu = wguT.rearrange("(c p) f -> p c f", p=128)
    for hgrp in range(16):
        for gv in range(2):
            cbase = (gv * 64 + hgrp * 4) * 128
            for k in range(KD):
                ternary_slab(flatgu, k, cbase, wq512[k], ("wq", k))
            for hh in range(4):
                hc = hgrp * 4 + hh
                for th in range(2):
                    bank = th
                    for k in range(KD):
                        zmov = PV((k * 8 + th * 4) * 0.25, 128, 512, BF16)
                        p.emit("tensor",
                               mmf(pm[bank][:],
                                   wq512[k].bitcast(BF16)[:, hh * 128:(hh + 1) * 128],
                                   zmov, k == 0, k == KD - 1),
                               rd=[("wq", k)] + [("zT", k, th * 4 + q)
                                                 for q in range(4)],
                               wr=[("pm", bank)], tag="mm_gu", nsw=(k != 0))
                if gv == 0:
                    for th in range(2):
                        p.emit("scalar",
                               act(scrGg[:, (hh * 2 + th) * 512:
                                         (hh * 2 + th + 1) * 512],
                                   pm[th][:], AF.Silu),
                               rd=[("pm", th)], wr=[("sgg", hh, th)])
                else:
                    for th in range(2):
                        p.emit("vector",
                               tt(h2_t(hc)[:, th * 512:(th + 1) * 512],
                                  scrGg[:, (hh * 2 + th) * 512:
                                        (hh * 2 + th + 1) * 512],
                                  pm[th][:], ALU.mult),
                               rd=[("sgg", hh, th), ("pm", th)],
                               wr=[("h2", hc, th)])
                        p.emit("scalar", act(h2sq,
                                             h2_t(hc)[:, th * 512:(th + 1) * 512],
                                             AF.Square), rd=[("h2", hc, th)],
                               wr=[("h2s",)])
                        p.emit("tensor", mmf(pm[2 + th][0:1, 0:512], onesb, h2sq,
                                             hc == 0, hc == HFF_K - 1),
                               rd=[("h2s",), ("const",)], wr=[("ssp", th)],
                               tag="mm_ss", nsw=(hc != 0))
    p.emit("vector", cp(ssrow[:, 0:512], pm[2][0:1, 0:512]), rd=[("ssp", 0)],
           wr=[("ssr",)])
    p.emit("vector", cp(ssrow[:, 512:1024], pm[3][0:1, 0:512]), rd=[("ssp", 1)],
           wr=[("ssr",)])
    p.emit("vector", ts(rr3r, ssrow, 1.0 / HFF), rd=[("ssr",)], wr=[("r3r",)])
    p.emit("scalar", act(rr3r, rr3r, AF.Ln, bias=cEPSF[0:1]), rd=[("r3r",)],
           wr=[("r3r",)])
    p.emit("scalar", act(rr3r, rr3r, AF.Exp, scale=-0.5), rd=[("r3r",)],
           wr=[("r3r",)])
    for tc in range(NTC):
        p.emit("tensor", trf(pq[:, 8 + tc:9 + tc],
                             rr3r[:, tc * 128:(tc + 1) * 128], identf[0:1, 0:1]),
               rd=[("r3r",), ("const",)] + ([("pq",)] if tc == 0 else []),
               wr=[("pq",)], nsw=(tc != 0))
    p.emit("vector", cp(rr3c[:, 0:8], pq[:, 8:16]), rd=[("pq",)],
           wr=[("rr3", tc) for tc in range(NTC)])

    if upto <= 8:
        return _close(nc, p, TAPS, tap_names)

    p.fence()
    prsl = [PV(103, 128, 512, U16), PV(104, 128, 512, U16)]
    xo1rd = [PV(105, 128, 512, F32), PV(107, 128, 512, F32)]
    outsc = PV(109, 128, 512, F32)
    flatpr = wprT.rearrange("(c p) f -> p c f", p=128)
    for w in range(4):
        for g in range(2):
            tcs = list(range(g * 4, g * 4 + 4))
            for k in range(HFF_K):
                sb = k % 2
                if g == 0:
                    ternary_slab(flatpr, k, w * 512, prsl[sb], ("wpr", sb))
                    p.dma("sync", wprq_hbm[w, k], prsl[sb].bitcast(BF16),
                          rd=[("wpr", sb)], wr=[("wpq", w, k)], tag="w_prq_out")
                else:
                    p.dma("sync", prsl[sb].bitcast(BF16), wprq_hbm[w, k],
                          rd=[("wpq", w, k)], wr=[("wpr", sb)], tag="w_prq_in")
                for ti, tc in enumerate(tcs):
                    p.emit("tensor", mmf(pm[ti][:],
                                         h2_t(k)[:, tc * 128:(tc + 1) * 128],
                                         prsl[sb].bitcast(BF16),
                                         k == 0, k == HFF_K - 1),
                           rd=[("wpr", sb), ("h2", k, tc // 4)],
                           wr=[("pm", ti)], tag="mm_pr", nsw=(k != 0))
            for ti, tc in enumerate(tcs):
                xb = ti % 2
                p.dma("sync", xo1rd[xb],
                      xo1_hbm[tc * 128:(tc + 1) * 128, w * 512:(w + 1) * 512],
                      rd=[("xoh", tc)], wr=[("xor", xb)], tag="xo1_rd")
                p.emit("vector", ts(outsc, pm[ti][:], rr3c[:, tc:tc + 1]),
                       rd=[("pm", ti), ("rr3", tc)], wr=[("ousc",)])
                p.emit("vector", tt(outsc, outsc, xo1rd[xb], ALU.add),
                       rd=[("ousc",), ("xor", xb)], wr=[("ousc",)])
                p.dma("sync", out_hbm[tc * 128:(tc + 1) * 128,
                                      w * 512:(w + 1) * 512],
                      outsc, rd=[("ousc",)], wr=[("outd", tc, w)], tag="out")

    return _close(nc, p, TAPS, tap_names)


def _close(nc, p, TAPS, tap_names):
    p.fence()
    for name in tap_names:
        src = TAPS.get(name)
        if src is None:
            continue
        ap_, rows = src
        dreg = nc.dram_tensor("dbg_" + name, [rows, ap_.shape[-1]], ap_.dtype,
                              kind="ExternalOutput").ap()
        p.dma("sync", dreg, ap_, tag="dbg")

    from contextlib import ExitStack
    stack = ExitStack()
    sems = {sn: stack.enter_context(nc.semaphore("sem_" + sn))
            for sn in p.semnames}
    block = stack.enter_context(nc.Block())
    final = [(sn, p.count[sn]) for sn in p.semnames if p.count[sn] > 0]

    def make(engname):
        def body(engobj):
            water = {}
            for fn, deps, own_sem, inc, tag in p.q[engname]:
                need = {}
                for (de, dv) in deps:
                    if dv > water.get(de, 0):
                        need[de] = max(need.get(de, 0), dv)
                for de, dv in need.items():
                    engobj.wait_ge(sems[de], dv)
                    water[de] = dv
                instr = fn(engobj)
                instr.then_inc(sems[own_sem], inc)
            if engname == "sync":
                for (de, dv) in final:
                    engobj.wait_ge(sems[de], dv)
        return body

    block.sync(make("sync"))
    block.vector(make("vector"))
    block.scalar(make("scalar"))
    block.tensor(make("tensor"))
    block.gpsimd(make("gpsimd"))
    stack.close()
    return nc, p


def _rope_tables(core):
    inv_freq = (1.0 / (np.float32(10000.0) **
                       (np.arange(0, HD, 2, dtype=np.float32) / np.float32(HD))
                       )).astype(np.float32)
    pos = (1024 * (core % 2) + np.arange(T)).astype(np.float32)
    freqs = np.outer(pos, inv_freq).astype(np.float32)
    return (np.cos(freqs).astype(np.float32), np.sin(freqs).astype(np.float32))


def _mask0():
    m = np.zeros((128, 128), np.float32)
    r = np.arange(128)
    m[r[:, None] < r[None, :]] = -3.0e38
    return m


def _mask1():
    m = np.zeros((128, 256), np.float32)
    r = np.arange(128)
    c = np.arange(128)
    sub = np.zeros((128, 128), np.float32)
    sub[r[:, None] < c[None, :]] = -3.0e38
    m[:, 128:] = sub
    return m


def _gsel():
    g = np.zeros((128, 2), BF)
    g[0:64, 0] = 1
    g[64:128, 1] = 1
    return g


def _gexp():
    g = np.zeros((2, 128), BF)
    g[0, 0:64] = 1
    g[1, 64:128] = 1
    return g


def host_prepare(inputs):
    x = np.ascontiguousarray(np.asarray(inputs["L0_attn_x"], dtype=np.float32)
                             .reshape(8 * T, D))
    f32 = np.float32
    shared = {
        "wqkvT": np.ascontiguousarray(np.asarray(inputs["L0_attn_qkv_w"], f32).T),
        "woT": np.ascontiguousarray(np.asarray(inputs["L0_attn_o_w"], f32).T),
        "wupT": np.ascontiguousarray(np.asarray(inputs["L0_attn_ffn_up_w"], f32).T),
        "wdnT": np.ascontiguousarray(np.asarray(inputs["L0_attn_ffn_down_w"], f32).T),
        "wguT": np.ascontiguousarray(np.asarray(inputs["L0_ffn_w_gate_up"], f32).T),
        "wprT": np.ascontiguousarray(np.asarray(inputs["L0_ffn_w_proj"], f32).T),
        "anw": np.ascontiguousarray(np.broadcast_to(
            np.asarray(inputs["L0_attn_attn_norm_w"], f32), (128, D))),
        "fnw": np.ascontiguousarray(np.broadcast_to(
            np.asarray(inputs["L0_attn_ffn_norm_w"], f32), (128, D))),
        "mask0": _mask0(), "mask1": _mask1(),
        "identb": np.eye(128, dtype=BF), "identf": np.eye(128, dtype=np.float32),
        "gsel": _gsel(), "gexp": _gexp(),
    }
    cores = []
    for c in range(8):
        cos_t, sin_t = _rope_tables(c)
        d = dict(shared)
        d["x"] = np.ascontiguousarray(x[c * T:(c + 1) * T])
        d["cos_t"] = cos_t
        d["sin_t"] = sin_t
        cores.append(d)
    return cores


def kernel(**inputs):
    cores = host_prepare(inputs)
    nc, p = build_program()
    res = run_bass_kernel_spmd(nc, cores, core_ids=list(range(8)))
    out = np.concatenate([r["out"] for r in res.results], axis=0)
    return np.ascontiguousarray(out.reshape(4, 2048, 2048))


# version 03
# speedup vs baseline: 1.3406x; 1.3406x over previous
import numpy as np
import ml_dtypes
import concourse.bass as bass
import concourse.mybir as mybir
from concourse.bass_utils import run_bass_kernel_spmd

F32 = mybir.dt.float32
BF16 = mybir.dt.bfloat16
I32 = mybir.dt.int32
U8 = mybir.dt.uint8
U16 = mybir.dt.uint16
AF = mybir.ActivationFunctionType
ALU = mybir.AluOpType
AX = mybir.AxisListType
BF = ml_dtypes.bfloat16

T, D, NH, HD = 1024, 2048, 16, 128
NTC, KD = 8, 16
NBLK = 4
QKV_N, UP_N, INTER, DN_K = 6144, 8192, 4096, 32
GU_N, HFF, HFF_K = 16384, 8192, 64
F32EPS = float(np.finfo(np.float32).eps)
RSC = float(HD ** -0.5)
ARENA_KB = 204
KB = 1024
MAGIC = 1.5 * 2.0 ** 23

ENGINES = ("sync", "vector", "scalar", "tensor", "gpsimd")
DMA_Q = ("sync", "gpsimd")


class Plan:
    NPOOL = {"sync": 12, "gpsimd": 6}

    def __init__(self):
        self.q = {e: [] for e in ENGINES}
        self.semnames = [e for e in ENGINES if e not in DMA_Q]
        for qn, n in self.NPOOL.items():
            self.semnames += ["%s%d" % (qn, i) for i in range(n)]
        self.count = {sn: 0 for sn in self.semnames}
        self.dma_idx = {qn: 0 for qn in self.NPOOL}
        self.res = {}
        self.pending_fence = {e: [] for e in ENGINES}

    def fence(self):
        snap = [(sn, self.count[sn]) for sn in self.semnames if self.count[sn] > 0]
        for e in ENGINES:
            self.pending_fence[e] = list(snap)

    def _gather(self, eng, rd, wr, extra):
        deps = [t for t in extra if t is not None]
        if self.pending_fence[eng]:
            deps += self.pending_fence[eng]
            self.pending_fence[eng] = []
        for k in rd:
            e = self.res.get(k)
            if e and e[0]:
                deps.append(e[0])
        for k in wr:
            e = self.res.get(k)
            if e:
                if e[0]:
                    deps.append(e[0])
                deps.extend(e[1])
        return deps

    def _note(self, tok, rd, wr):
        for k in rd:
            self.res.setdefault(k, [None, []])[1].append(tok)
        for k in wr:
            self.res[k] = [tok, []]
        return tok

    def emit(self, eng, fn, rd=(), wr=(), extra=(), inc=1, tag="", nsw=False):
        assert eng not in DMA_Q
        deps = self._gather(eng, rd, wr, extra)
        if nsw:
            deps = [d for d in deps if d[0] != eng]
        self.q[eng].append((fn, deps, eng, 1, tag))
        self.count[eng] += 1
        return self._note((eng, self.count[eng]), rd, wr)

    def dma(self, queue, out_ap, in_ap, rd=(), wr=(), extra=(), tag=""):
        fn = (lambda o, i: lambda e: e.dma_start(out=o, in_=i))(out_ap, in_ap)
        sn = "%s%d" % (queue, self.dma_idx[queue] % self.NPOOL[queue])
        self.dma_idx[queue] += 1
        deps = self._gather(queue, rd, wr, extra)
        if self.count[sn] > 0:
            deps.append((sn, self.count[sn]))
        self.q[queue].append((fn, deps, sn, 16, tag))
        self.count[sn] += 16
        return self._note((sn, self.count[sn]), rd, wr)


def build_program(upto=99, tap_names=()):
    nc = bass.Bass("TRN2", target_bir_lowering=False)

    x_in = nc.dram_tensor("x", [T, D], F32, kind="ExternalInput").ap()
    wqkvT = nc.dram_tensor("wqkvT", [D, QKV_N], F32, kind="ExternalInput").ap()
    woT = nc.dram_tensor("woT", [D, D], F32, kind="ExternalInput").ap()
    wupT = nc.dram_tensor("wupT", [D, UP_N], F32, kind="ExternalInput").ap()
    wdnT = nc.dram_tensor("wdnT", [INTER, D], F32, kind="ExternalInput").ap()
    wguT = nc.dram_tensor("wguT", [D, GU_N], F32, kind="ExternalInput").ap()
    wprT = nc.dram_tensor("wprT", [HFF, D], F32, kind="ExternalInput").ap()
    anw_in = nc.dram_tensor("anw", [128, D], F32, kind="ExternalInput").ap()
    fnw_in = nc.dram_tensor("fnw", [128, D], F32, kind="ExternalInput").ap()
    cos_in = nc.dram_tensor("cos_t", [T, 64], F32, kind="ExternalInput").ap()
    sin_in = nc.dram_tensor("sin_t", [T, 64], F32, kind="ExternalInput").ap()
    mask0_in = nc.dram_tensor("mask0", [128, 128], F32, kind="ExternalInput").ap()
    mask1_in = nc.dram_tensor("mask1", [128, 256], F32, kind="ExternalInput").ap()
    idb_in = nc.dram_tensor("identb", [128, 128], BF16, kind="ExternalInput").ap()
    idf_in = nc.dram_tensor("identf", [128, 128], F32, kind="ExternalInput").ap()
    gsel_in = nc.dram_tensor("gsel", [128, 2], BF16, kind="ExternalInput").ap()
    gexp_in = nc.dram_tensor("gexp", [2, 128], BF16, kind="ExternalInput").ap()
    out_hbm = nc.dram_tensor("out", [T, D], F32, kind="ExternalOutput").ap()
    y_hbm = nc.dram_tensor("y_hbm", [T, D], F32, kind="Internal").ap()
    v_hbm = nc.dram_tensor("v_hbm", [T, D], BF16, kind="Internal").ap()
    x2_hbm = nc.dram_tensor("x2_hbm", [T, D], F32, kind="Internal").ap()
    xo1_hbm = nc.dram_tensor("xo1_hbm", [T, D], F32, kind="Internal").ap()
    wdnq_hbm = nc.dram_tensor("wdnq_hbm", [4, 8, 128, 2048], BF16, kind="Internal").ap()
    wprq_hbm = nc.dram_tensor("wprq_hbm", [4, 64, 128, 512], BF16, kind="Internal").ap()

    arena = nc.sbuf_tensor("arena", [128, ARENA_KB * KB], U8).__enter__()
    pm = [nc.psum_tensor("pm%d" % i, [128, 512], F32).__enter__() for i in range(5)]
    pq = nc.psum_tensor("pq", [128, 512], F32).__enter__()
    ptb_banks = [nc.psum_tensor("ptb%d" % i, [128, 1024], BF16).__enter__()
                 for i in range(2)]

    def V(off_b, rows, cols, dtype):
        sz = mybir.dt.size(dtype)
        o = int(off_b)
        assert o % sz == 0 and o + cols * sz <= ARENA_KB * KB, (o, cols, dtype)
        return arena[0:rows, o:o + cols * sz].bitcast(dtype)

    stats = V(0, 128, 448, F32)
    _sc = [0]
    def scol(n=1):
        c = _sc[0]; _sc[0] += n
        assert _sc[0] <= 448, "stats overflow"
        return stats[:, c:c + n]

    cosT = V(2 * KB, 128, 512, F32)
    sinT = V(4 * KB, 128, 512, F32)
    identb = V(6 * KB, 128, 128, BF16)
    identf = V(6400, 128, 128, F32)
    gsel = V(6912, 128, 2, BF16)
    gexp = V(6920, 2, 128, BF16)
    onesb = V(7180, 128, 1, BF16)
    onesf = V(7184, 128, 1, F32)
    mrow = V(7192, 1, 2, F32)
    brow = V(7208, 1, 128, F32)
    mask0 = V(8 * KB, 128, 128, F32)
    mask1 = V(9 * KB, 128, 256, F32)
    POOL = 10 * KB

    def PV(off_kb, rows, cols, dtype):
        return V(POOL + int(off_kb * KB), rows, cols, dtype)

    p = Plan()
    TAPS = {}

    def ts(out_ap, in_ap, s1, s2=None, op0=ALU.mult, op1=None, accum=None):
        def f(e):
            kw = {"op1": op1} if op1 is not None else {}
            return e.tensor_scalar(out=out_ap, in0=in_ap, scalar1=s1, scalar2=s2,
                                   op0=op0, accum_out=accum, **kw)
        return f

    def tt(o, a, b, op):
        return lambda e: e.tensor_tensor(out=o, in0=a, in1=b, op=op)

    def red(o, i, op, absval=False):
        return lambda e: e.tensor_reduce(out=o, in_=i, axis=AX.X, op=op,
                                         apply_absolute_value=absval)

    def cp(o, i):
        return lambda e: e.tensor_copy(o, i)

    def recip(o, i):
        return lambda e: e.reciprocal(o, i)

    def act(o, i, func, bias=0.0, scale=1.0, accum=None):
        return lambda e: e.activation(out=o, in_=i, func=func, bias=bias,
                                      scale=scale, accum_out=accum)

    def mmf(o, lhsT, rhs, start, stop):
        return lambda e: e.matmul(o, lhsT, rhs, start=start, stop=stop)

    def trf(o, i, ident):
        return lambda e: e.transpose(o, i, ident)

    _tb = [0]

    def transpose_group(items, dst_ap, dst_keys, extra_rd=()):
        assert len(items) <= 8
        b = _tb[0] % 2
        _tb[0] += 1
        pt = ptb_banks[b]
        for j, (sap, skeys) in enumerate(items):
            p.emit("tensor", trf(pt[:, j * 128:(j + 1) * 128], sap, identb),
                   rd=list(skeys) + [("const",)], wr=[("ptbank", b)],
                   nsw=True, tag="tr")
        return p.emit("vector", cp(dst_ap, pt[:, 0:len(items) * 128]),
                      rd=[("ptbank", b)] + list(extra_rd), wr=list(dst_keys))

    for dst, src in ((cosT, cos_in), (sinT, sin_in)):
        p.dma("sync", dst.rearrange("p (c f) -> p c f", c=8),
              src.rearrange("(c p) f -> p c f", p=128),
              wr=[("const",)], tag="c")
    for dst, src in ((identb, idb_in), (identf, idf_in), (gsel, gsel_in),
                     (gexp, gexp_in), (mask0, mask0_in), (mask1, mask1_in)):
        p.dma("sync", dst, src, wr=[("const",)], tag="c")
    p.emit("vector", lambda e: e.memset(onesb, 1.0), wr=[("const",)])
    p.emit("vector", lambda e: e.memset(onesf, 1.0), wr=[("const",)])
    cEPS5 = V(7200, 128, 1, F32)
    cEPSF = V(7204, 128, 1, F32)
    p.emit("vector", lambda e: e.memset(cEPS5, 1e-5), wr=[("const",)])
    p.emit("vector", lambda e: e.memset(cEPSF, F32EPS), wr=[("const",)])

    wfs = [PV(96, 128, 4096, F32), PV(112, 128, 4096, F32)]
    wqh = [PV(128, 128, 4096, BF16), PV(136, 128, 4096, BF16)]
    qa_s = PV(146, 128, 4096, BF16)
    qb_s = PV(154, 128, 4096, BF16)
    sinkb = PV(146, 128, 2048, BF16)
    msum = scol(96)
    inv_sw, sw_pp = {}, {}
    WSPEC = {"qkv": (wqkvT, D, QKV_N), "o": (woT, D, D), "dn": (wdnT, INTER, D)}
    coli = 0
    for name, (hbm, rows, cols) in WSPEC.items():
        flat = hbm.rearrange("(c p) f -> p c f", p=128)
        c0 = coli
        for r in range(rows // 128):
            for cc in range(0, cols, 2048):
                b = coli % 2
                p.dma("sync", wfs[b][:, 0:2048], flat[:, r, cc:cc + 2048],
                      wr=[("wfs", b)], tag="mean")
                p.emit("scalar", act(sinkb, wfs[b][:, 0:2048], AF.Abs,
                                     accum=msum[:, coli:coli + 1]),
                       rd=[("wfs", b)], wr=[("sink",), ("ms", coli)])
                coli += 1
        csum = scol()
        p.emit("vector", red(csum, msum[:, c0:coli], ALU.add),
               rd=[("ms", i) for i in range(c0, coli)], wr=[("cs", name)])
        p.emit("tensor", mmf(pq[0:1, 0:1], csum, onesf, True, True),
               rd=[("cs", name), ("const",)], wr=[("pq",)])
        p.emit("vector", ts(mrow[:, 0:1], pq[0:1, 0:1], 1.0 / (rows * cols), 1e-5,
                            ALU.mult, ALU.max), rd=[("pq",)], wr=[("mrow",)])
        p.emit("vector", cp(brow, mrow[:, 0:1].broadcast_to([1, 128])),
               rd=[("mrow",)], wr=[("brow",)])
        p.emit("tensor", trf(pq[:, 2:3], brow, identf[0:1, 0:1]),
               rd=[("brow",), ("const",), ("pq",)], wr=[("pq",)])
        isw = scol()
        p.emit("vector", cp(isw, pq[:, 2:3]), rd=[("pq",)], wr=[("isw", name)])
        swc = scol()
        p.emit("vector", recip(swc, isw), rd=[("isw", name)], wr=[("sw", name)])
        inv_sw[name] = isw
        sw_pp[name] = swc

    def qbit(name, wfs_ap, wqh_ap, bkey, k0, c0, nk):
        hbm, rows, cols = WSPEC[name]
        flat = hbm.rearrange("(c p) f -> p c f", p=128)
        w = nk * 512
        p.dma("sync", wfs_ap.rearrange("p (k f) -> p k f", k=nk),
              flat[:, k0:k0 + nk, c0:c0 + 512], wr=[bkey], tag="w_" + name)
        p.emit("vector", ts(qa_s[:, 0:w], wfs_ap[:, 0:w], sw_pp[name], 0.5,
                            ALU.mult, ALU.is_gt),
               rd=[bkey, ("sw", name)], wr=[("qa",)])
        p.emit("vector", ts(qb_s[:, 0:w], wfs_ap[:, 0:w], sw_pp[name], -0.5,
                            ALU.mult, ALU.is_lt),
               rd=[bkey, ("sw", name)], wr=[("qb",)])
        p.emit("vector", tt(wqh_ap[:, 0:w], qa_s[:, 0:w], qb_s[:, 0:w],
                            ALU.subtract),
               rd=[("qa",), ("qb",)],
               wr=[(bkey, kk) for kk in range(nk)])

    TAPS["msum"] = (stats, 128)
    if upto <= 0:
        return _close(nc, p, TAPS, tap_names)

    p.fence()
    def hqT_t(k, tc):
        return PV((k * 8 + tc) * 0.25, 128, 128, BF16)
    t1s = PV(32, 128, D, F32)
    hqi = PV(40, 128, D, I32)
    hqb = PV(48, 128, D, BF16)
    sqs = PV(52, 128, D, BF16)
    anw_s = PV(56, 128, D, F32)
    y_s = PV(64, 128, D, F32)
    x_s = [PV(72, 128, D, F32), PV(80, 128, D, F32)]
    inv_sh = scol(8)

    def ln_stats(tc, x_ap, xkey, pref, sink_ap, sink_key):
        s1, ssq, m_, var_, al, nb = scol(), scol(), scol(), scol(), scol(), scol()
        p.emit("vector", red(s1, x_ap, ALU.add), rd=xkey, wr=[(pref + "s", tc)])
        p.emit("scalar", act(sink_ap, x_ap, AF.Square, accum=ssq),
               rd=xkey, wr=[(pref + "q", tc), sink_key])
        p.emit("vector", ts(m_, s1, 1.0 / D), rd=[(pref + "s", tc)],
               wr=[(pref + "m", tc)])
        p.emit("vector", ts(var_, m_, m_), rd=[(pref + "m", tc)],
               wr=[(pref + "v", tc)])
        p.emit("vector", ts(ssq, ssq, 1.0 / D), rd=[(pref + "q", tc)],
               wr=[(pref + "q", tc)])
        p.emit("vector", tt(var_, ssq, var_, ALU.subtract),
               rd=[(pref + "q", tc), (pref + "v", tc)], wr=[(pref + "v", tc)])
        p.emit("scalar", act(al, var_, AF.Ln, bias=cEPS5), rd=[(pref + "v", tc)],
               wr=[(pref + "a", tc)])
        p.emit("scalar", act(al, al, AF.Exp, scale=-0.5), rd=[(pref + "a", tc)],
               wr=[(pref + "a", tc)])
        p.emit("vector", ts(nb, m_, al, -1.0, ALU.mult, ALU.mult),
               rd=[(pref + "m", tc), (pref + "a", tc)], wr=[(pref + "n", tc)])
        return al, nb, var_, m_

    p.dma("sync", anw_s, anw_in, wr=[("anw",)], tag="anw")
    for tc in range(NTC):
        b = tc % 2
        p.dma("sync", x_s[b], x_in[tc * 128:(tc + 1) * 128, :], wr=[("xs", b)],
              tag="x")
        xkey = [("xs", b)]
        al1, nb1, var1, m1c = ln_stats(tc, x_s[b], xkey, "l1", sqs, ("sqs",))
        r2_, tmp = scol(), scol()
        p.emit("vector", ts(tmp, al1, al1), rd=[("l1a", tc)], wr=[("tm", tc)])
        p.emit("vector", tt(tmp, var1, tmp, ALU.mult),
               rd=[("l1v", tc), ("tm", tc)], wr=[("tm", tc)])
        p.emit("scalar", act(r2_, tmp, AF.Ln, bias=cEPS5), rd=[("tm", tc)],
               wr=[("r2", tc)])
        p.emit("scalar", act(r2_, r2_, AF.Exp, scale=-0.5), rd=[("r2", tc)],
               wr=[("r2", tc)])
        p.emit("vector", ts(y_s, x_s[b], m1c, None, ALU.subtract),
               rd=xkey + [("l1m", tc)], wr=[("ys",)])
        p.emit("vector", ts(y_s, y_s, al1), rd=[("ys",), ("l1a", tc)], wr=[("ys",)])
        p.dma("sync", y_hbm[tc * 128:(tc + 1) * 128, :], y_s, rd=[("ys",)],
              wr=[("yh", tc)], tag="y_out")
        p.emit("vector", tt(t1s, y_s, anw_s, ALU.mult), rd=[("ys",), ("anw",)],
               wr=[("t1",)])
        m1_, mh, c_ = scol(), scol(), scol()
        p.emit("vector", red(m1_, t1s, ALU.max, absval=True), rd=[("t1",)],
               wr=[("m1", tc)])
        p.emit("vector", ts(mh, m1_, r2_, 1e-5, ALU.mult, ALU.max),
               rd=[("m1", tc), ("r2", tc)], wr=[("mhc", tc)])
        p.emit("vector", ts(inv_sh[:, tc:tc + 1], mh, 1.0 / 127.0),
               rd=[("mhc", tc)], wr=[("ish", tc)])
        p.emit("vector", recip(c_, mh), rd=[("mhc", tc)], wr=[("chc", tc)])
        p.emit("vector", ts(c_, c_, r2_, 127.0, ALU.mult, ALU.mult),
               rd=[("r2", tc), ("chc", tc)], wr=[("chc", tc)])
        p.emit("vector", ts(t1s, t1s, c_, MAGIC, ALU.mult, ALU.add),
               rd=[("chc", tc), ("t1",)], wr=[("t1",)])
        p.emit("vector", ts(t1s, t1s, MAGIC, None, ALU.subtract),
               rd=[("t1",)], wr=[("t1",)])
        p.emit("vector", cp(hqi, t1s), rd=[("t1",)], wr=[("hqi",)])
        p.emit("scalar", act(hqb, hqi, AF.Copy), rd=[("hqi",)], wr=[("hqb",)])
        hqT_view = PV(0, 128, 16384, BF16).rearrange(
            "p (k t c) -> p k t c", k=16, t=8)
        for k0 in (0, 8):
            transpose_group(
                [(hqb[:, k * 128:(k + 1) * 128], [("hqb",)])
                 for k in range(k0, k0 + 8)],
                hqT_view[:, k0:k0 + 8, tc, :],
                [("hqT", k, tc) for k in range(k0, k0 + 8)])

    TAPS["hqT"] = (PV(0, 128, 16384, BF16), 128)
    TAPS["hqb"] = (hqb, 128)
    TAPS["stats"] = (stats, 128)
    if upto <= 1:
        return _close(nc, p, TAPS, tap_names)

    p.fence()
    def qT_t(h, tc):
        return PV(32 + (h * 8 + tc) * 0.25, 128, 128, BF16)
    def kT_t(h, tc, w=128):
        return PV(64 + (h * 8 + tc) * 0.25, 128, w, BF16)
    ropeA = PV(162, 128, 256, F32)
    ropeB = PV(163, 128, 256, F32)
    ropeQ = PV(164, 128, 512, BF16)
    vstage = PV(165, 128, 512, BF16)
    cos_sc = PV(166, 128, 64, F32)
    sin_sc = PV(166.25, 128, 64, F32)
    a_q = scol(8)
    for tc in range(NTC):
        p.emit("vector", ts(a_q[:, tc:tc + 1], inv_sh[:, tc:tc + 1], inv_sw["qkv"]),
               rd=[("ish", tc), ("isw", "qkv")], wr=[("aq", tc)])
    for w in range(12):
        qbit("qkv", wfs[0], wqh[0], ("wb", 0), 0, w * 512, 8)
        qbit("qkv", wfs[1], wqh[1], ("wb", 1), 8, w * 512, 8)
        for tc in range(NTC):
            bank = (w * NTC + tc) % 4
            for k in range(KD):
                b = k // 8
                p.emit("tensor", mmf(pm[bank][:], hqT_t(k, tc),
                                     wqh[b][:, (k % 8) * 512:(k % 8 + 1) * 512],
                                     k == 0, k == KD - 1),
                       rd=[("hqT", k, tc), (("wb", b), k % 8)], wr=[("pm", bank)],
                       tag="mm_qkv", nsw=(k != 0))
            if w < 8:
                is_q = w < 4
                h0 = (w % 4) * 4
                ps3 = pm[bank][:].rearrange("p (h f) -> p h f", h=4)
                p.emit("vector", ts(cos_sc, cosT[:, tc * 64:(tc + 1) * 64],
                                    a_q[:, tc:tc + 1]),
                       rd=[("aq", tc), ("const",)], wr=[("csc",)])
                p.emit("vector", ts(sin_sc, sinT[:, tc * 64:(tc + 1) * 64],
                                    a_q[:, tc:tc + 1]),
                       rd=[("aq", tc), ("const",)], wr=[("ssc",)])
                csb = cos_sc.unsqueeze(1).broadcast_to([128, 4, 64])
                snb = sin_sc.unsqueeze(1).broadcast_to([128, 4, 64])
                rA = ropeA.rearrange("p (h f) -> p h f", h=4)
                rB = ropeB.rearrange("p (h f) -> p h f", h=4)
                rQ = ropeQ.rearrange("p (h f) -> p h f", h=4)
                p.emit("vector", tt(rA, ps3[:, :, 0:64], csb, ALU.mult),
                       rd=[("pm", bank), ("csc",)], wr=[("rA",)])
                p.emit("vector", tt(rB, ps3[:, :, 64:128], snb, ALU.mult),
                       rd=[("pm", bank), ("ssc",)], wr=[("rB",)])
                p.emit("vector", tt(rQ[:, :, 0:64], rA, rB, ALU.subtract),
                       rd=[("rA",), ("rB",)], wr=[("rQ",)])
                p.emit("vector", tt(rA, ps3[:, :, 64:128], csb, ALU.mult),
                       rd=[("pm", bank), ("csc",), ("rQ",)], wr=[("rA",)])
                p.emit("vector", tt(rB, ps3[:, :, 0:64], snb, ALU.mult),
                       rd=[("pm", bank), ("ssc",), ("rQ",)], wr=[("rB",)])
                ev = p.emit("vector", tt(rQ[:, :, 64:128], rA, rB, ALU.add),
                            rd=[("rA",), ("rB",)], wr=[("rQ",)])
                base = 32 if is_q else 64
                dview = PV(base, 128, 16384, BF16).rearrange(
                    "p (h t c) -> p h t c", h=16, t=8)
                transpose_group(
                    [(ropeQ[:, hh * 128:(hh + 1) * 128], [("rQ",)])
                     for hh in range(4)],
                    dview[:, h0:h0 + 4, tc, :],
                    [("qT" if is_q else "kT", h0 + hh, tc) for hh in range(4)])
            else:
                vcol = (w - 8) * 512
                p.emit("vector", ts(vstage, pm[bank][:], a_q[:, tc:tc + 1]),
                       rd=[("pm", bank), ("aq", tc)], wr=[("vst",)])
                p.dma("sync", v_hbm[tc * 128:(tc + 1) * 128, vcol:vcol + 512],
                      vstage, rd=[("vst",)], wr=[("vh", tc)], tag="v_out")

    TAPS["qT"] = (PV(32, 128, 16384, BF16), 128)
    TAPS["kT"] = (PV(64, 128, 16384, BF16), 128)
    if upto <= 2:
        return _close(nc, p, TAPS, tap_names)

    p.fence()
    def ctxqT_t(k, tc):
        return PV((k * 8 + tc) * 0.25, 128, 128, BF16)
    v_rd = [PV(144, 128, D, BF16), PV(148, 128, D, BF16)]
    scrS = PV(152, 128, 384, F32)
    P0t = PV(154, 128, 128, BF16)
    P1t = PV(154.25, 128, 256, BF16)
    ctx_f = [PV(155, 128, D, F32), PV(163, 128, D, F32)]
    ctxq_i = PV(171, 128, D, I32)
    ctxq_b = PV(179, 128, D, BF16)
    inv_sctx = scol(8)
    rmaxc, rsumc, rrec, biasc = scol(2), scol(2), scol(2), scol(2)
    for b in range(NBLK):
        tc0, tc1 = 2 * b, 2 * b + 1
        for i, tcx in enumerate((tc0, tc1)):
            p.dma("sync", v_rd[i], v_hbm[tcx * 128:(tcx + 1) * 128, :],
                  rd=[("vh", tcx)], wr=[("vrd", i)], tag="v_rd")
        for h in range(NH):
            p.emit("tensor", mmf(pm[4][:, 0:128], qT_t(h, tc0), kT_t(h, tc0),
                                 True, True),
                   rd=[("qT", h, tc0), ("kT", h, tc0)], wr=[("pm", 4)], tag="mm_s")
            p.emit("tensor", mmf(pm[3][:, 0:256], qT_t(h, tc1),
                                 kT_t(h, tc0, 256), True, True),
                   rd=[("qT", h, tc1), ("kT", h, tc0), ("kT", h, tc1)],
                   wr=[("pm", 3)], tag="mm_s")
            p.emit("vector", tt(scrS[:, 0:128], pm[4][:, 0:128], mask0, ALU.add),
                   rd=[("pm", 4), ("const",)], wr=[("sS0",)])
            p.emit("vector", tt(scrS[:, 128:384], pm[3][:, 0:256], mask1,
                                ALU.add), rd=[("pm", 3), ("const",)], wr=[("sS1",)])
            p.emit("vector", red(rmaxc[:, 0:1], scrS[:, 0:128], ALU.max),
                   rd=[("sS0",)], wr=[("rm",)])
            p.emit("vector", red(rmaxc[:, 1:2], scrS[:, 128:384], ALU.max),
                   rd=[("sS1",)], wr=[("rm",)])
            p.emit("vector", ts(biasc, rmaxc, -RSC), rd=[("rm",)], wr=[("bia",)])
            p.emit("scalar", (lambda o_, i_, bi_, ac_: lambda e: e.activation(
                out=o_, in_=i_, func=AF.Exp, bias=bi_, scale=RSC, accum_out=ac_))(
                P0t, scrS[:, 0:128], biasc[:, 0:1], rsumc[:, 0:1]),
                rd=[("sS0",), ("bia",)], wr=[("P0",), ("rs",)])
            p.emit("scalar", (lambda o_, i_, bi_, ac_: lambda e: e.activation(
                out=o_, in_=i_, func=AF.Exp, bias=bi_, scale=RSC, accum_out=ac_))(
                P1t, scrS[:, 128:384], biasc[:, 1:2], rsumc[:, 1:2]),
                rd=[("sS1",), ("bia",)], wr=[("P1",), ("rs",)])
            p.emit("vector", recip(rrec, rsumc), rd=[("rs",)], wr=[("rr",)])
            transpose_group([(P0t, [("P0",)])], P0t, [("P0",)])
            transpose_group([(P1t[:, 0:128], [("P1",)]),
                             (P1t[:, 128:256], [("P1",)])], P1t, [("P1",)])
            p.emit("tensor", mmf(pm[0][:, 0:128], P0t,
                                 v_rd[0][:, h * 128:(h + 1) * 128], True, True),
                   rd=[("P0",), ("vrd", 0)], wr=[("pm", 0)], tag="mm_c")
            p.emit("tensor", mmf(pm[1][:, 0:128], P1t[:, 0:128],
                                 v_rd[0][:, h * 128:(h + 1) * 128], True, False),
                   rd=[("P1",), ("vrd", 0)], wr=[("pm", 1)], tag="mm_c")
            p.emit("tensor", mmf(pm[1][:, 0:128], P1t[:, 128:256],
                                 v_rd[1][:, h * 128:(h + 1) * 128], False, True),
                   rd=[("P1",), ("vrd", 1)], wr=[("pm", 1)], tag="mm_c", nsw=True)
            p.emit("vector", ts(ctx_f[0][:, h * 128:(h + 1) * 128], pm[0][:, 0:128],
                                rrec[:, 0:1]), rd=[("pm", 0), ("rr",)],
                   wr=[("cf0", h)])
            p.emit("vector", ts(ctx_f[1][:, h * 128:(h + 1) * 128],
                                pm[1][:, 0:128], rrec[:, 1:2]),
                   rd=[("pm", 1), ("rr",)], wr=[("cf1", h)])
        for qi, tcx in enumerate((tc0, tc1)):
            rkeys = [("cf%d" % qi, h) for h in range(NH)]
            mc, cc = scol(), scol()
            p.emit("vector", red(mc, ctx_f[qi], ALU.max, absval=True), rd=rkeys,
                   wr=[("mcx",)])
            p.emit("vector", ts(mc, mc, 1e-5, None, ALU.max), rd=[("mcx",)],
                   wr=[("mcx",)])
            p.emit("vector", ts(inv_sctx[:, tcx:tcx + 1], mc, 1.0 / 127.0),
                   rd=[("mcx",)], wr=[("isc", tcx)])
            p.emit("vector", recip(cc, mc), rd=[("mcx",)], wr=[("ccx",)])
            p.emit("vector", ts(cc, cc, 127.0), rd=[("ccx",)], wr=[("ccx",)])
            p.emit("vector", ts(ctx_f[qi], ctx_f[qi], cc, MAGIC, ALU.mult,
                                ALU.add), rd=[("ccx",)] + rkeys, wr=rkeys)
            p.emit("vector", ts(ctx_f[qi], ctx_f[qi], MAGIC, None, ALU.subtract),
                   rd=rkeys, wr=rkeys)
            p.emit("vector", cp(ctxq_i, ctx_f[qi]), rd=rkeys, wr=[("cqi",)])
            p.emit("scalar", act(ctxq_b, ctxq_i, AF.Copy), rd=[("cqi",)],
                   wr=[("cqb",)])
            cq_view = PV(0, 128, 16384, BF16).rearrange(
                "p (k t c) -> p k t c", k=16, t=8)
            for k0 in (0, 8):
                transpose_group(
                    [(ctxq_b[:, k * 128:(k + 1) * 128], [("cqb",)])
                     for k in range(k0, k0 + 8)],
                    cq_view[:, k0:k0 + 8, tcx, :],
                    [("cqT", k, tcx) for k in range(k0, k0 + 8)])

    TAPS["ctxqT"] = (PV(0, 128, 16384, BF16), 128)
    TAPS["ctxf0"] = (ctx_f[0], 128)
    TAPS["P0t"] = (P0t, 128)
    TAPS["P1t"] = (P1t, 128)
    TAPS["scrS"] = (scrS, 128)
    if upto <= 3:
        return _close(nc, p, TAPS, tap_names)

    p.fence()
    def y1_t(tc):
        return PV(32 + tc * 8, 128, D, F32)
    y_rd = [PV(162, 128, 512, F32), PV(164, 128, 512, F32)]
    oscr = PV(166, 128, 512, F32)
    a_o = scol(8)
    for tc in range(NTC):
        p.emit("vector", ts(a_o[:, tc:tc + 1], inv_sctx[:, tc:tc + 1],
                            inv_sw["o"]),
               rd=[("isc", tc), ("isw", "o")], wr=[("ao", tc)])
    for w in range(4):
        qbit("o", wfs[0], wqh[0], ("wb", 0), 0, w * 512, 8)
        qbit("o", wfs[1], wqh[1], ("wb", 1), 8, w * 512, 8)
        for tc in range(NTC):
            bank = (w * NTC + tc) % 4
            for k in range(KD):
                b = k // 8
                p.emit("tensor", mmf(pm[bank][:], ctxqT_t(k, tc),
                                     wqh[b][:, (k % 8) * 512:(k % 8 + 1) * 512],
                                     k == 0, k == KD - 1),
                       rd=[("cqT", k, tc), (("wb", b), k % 8)], wr=[("pm", bank)],
                       tag="mm_o", nsw=(k != 0))
            yb = tc % 2
            p.dma("sync", y_rd[yb],
                  y_hbm[tc * 128:(tc + 1) * 128, w * 512:(w + 1) * 512],
                  rd=[("yh", tc)], wr=[("yrd", yb)], tag="y_rd")
            p.emit("vector", ts(oscr, pm[bank][:], a_o[:, tc:tc + 1]),
                   rd=[("pm", bank), ("ao", tc)], wr=[("osc",)])
            p.emit("vector", tt(y1_t(tc)[:, w * 512:(w + 1) * 512], oscr,
                                y_rd[yb], ALU.add), rd=[("osc",), ("yrd", yb)],
                   wr=[("y1", tc, w)])

    TAPS["y1"] = (PV(32, 128, 16384, F32), 128)
    if upto <= 4:
        return _close(nc, p, TAPS, tap_names)

    p.fence()
    def hn1T_t(k, tc):
        return PV((k * 8 + tc) * 0.25, 128, 128, BF16)
    fnw_s = PV(152, 128, D, F32)
    hscr = PV(160, 128, D, F32)
    hb16 = PV(168, 128, D, BF16)
    sqs4 = PV(172, 128, D, BF16)
    xrd2 = [PV(176, 128, D, F32), PV(184, 128, D, F32)]
    p.dma("sync", fnw_s, fnw_in, wr=[("fnw",)], tag="fnw")
    for tc in range(NTC):
        y1k = [("y1", tc, w) for w in range(4)]
        ssq, rr2 = scol(), scol()
        p.emit("scalar", act(sqs4, y1_t(tc), AF.Square, accum=ssq),
               rd=y1k, wr=[("q2", tc), ("sqs4",)])
        p.emit("vector", ts(ssq, ssq, 1.0 / D), rd=[("q2", tc)], wr=[("q2", tc)])
        p.emit("scalar", act(rr2, ssq, AF.Ln, bias=cEPS5), rd=[("q2", tc)],
               wr=[("rr2", tc)])
        p.emit("scalar", act(rr2, rr2, AF.Exp, scale=-0.5), rd=[("rr2", tc)],
               wr=[("rr2", tc)])
        p.emit("vector", tt(hscr, y1_t(tc), fnw_s, ALU.mult), rd=y1k + [("fnw",)],
               wr=[("hscr",)])
        p.emit("vector", ts(hb16, hscr, rr2), rd=[("hscr",), ("rr2", tc)],
               wr=[("hb16",)])
        hn_view = PV(0, 128, 16384, BF16).rearrange(
            "p (k t c) -> p k t c", k=16, t=8)
        for k0 in (0, 8):
            transpose_group(
                [(hb16[:, k * 128:(k + 1) * 128], [("hb16",)])
                 for k in range(k0, k0 + 8)],
                hn_view[:, k0:k0 + 8, tc, :],
                [("hn1T", k, tc) for k in range(k0, k0 + 8)])
        xb = tc % 2
        p.dma("sync", xrd2[xb], x_in[tc * 128:(tc + 1) * 128, :], wr=[("xr2", xb)],
              tag="x_rd")
        p.emit("vector", tt(y1_t(tc), y1_t(tc), xrd2[xb], ALU.add),
               rd=y1k + [("xr2", xb)], wr=y1k)
        p.dma("sync", x2_hbm[tc * 128:(tc + 1) * 128, :], y1_t(tc), rd=y1k,
              wr=[("x2h", tc)], tag="x2_out")

    if upto <= 5:
        return _close(nc, p, TAPS, tap_names)

    p.fence()
    def sv_t(ti):
        return PV(32 + ti * 16, 128, INTER, F32)
    def svqT_t(k, tc):
        return PV(96 + (k * 8 + tc) * 0.25, 128, 128, BF16)
    upst = [PV(160, 128, 512, BF16), PV(161, 128, 512, BF16)]
    scrG4 = PV(162, 128, 2048, F32)
    svq_i = PV(170, 128, INTER, I32)
    svq_b = PV(186, 128, INTER, BF16)
    inv_ssv = scol(8)
    flatup = wupT.rearrange("(c p) f -> p c f", p=128)
    for g in range(2):
        tcs = list(range(g * 4, g * 4 + 4))
        for wp in range(8):
            for half, wofs in ((0, wp), (1, wp + 8)):
                for k in range(KD):
                    ub = k % 2
                    p.dma("gpsimd", upst[ub],
                          flatup[:, k, wofs * 512:(wofs + 1) * 512],
                          wr=[("upst", ub)], tag="w_up")
                    for ti, tc in enumerate(tcs):
                        p.emit("tensor", mmf(pm[ti][:], hn1T_t(k, tc), upst[ub],
                                             k == 0, k == KD - 1),
                               rd=[("hn1T", k, tc), ("upst", ub)],
                               wr=[("pm", ti)], tag="mm_up", nsw=(k != 0))
                if half == 0:
                    for ti, tc in enumerate(tcs):
                        p.emit("scalar", act(scrG4[:, ti * 512:(ti + 1) * 512],
                                             pm[ti][:], AF.Silu),
                               rd=[("pm", ti)], wr=[("sg", ti)])
                else:
                    for ti, tc in enumerate(tcs):
                        p.emit("vector", tt(sv_t(ti)[:, wp * 512:(wp + 1) * 512],
                                            scrG4[:, ti * 512:(ti + 1) * 512],
                                            pm[ti][:], ALU.mult),
                               rd=[("sg", ti), ("pm", ti)], wr=[("sv", ti, wp)])
        for ti, tc in enumerate(tcs):
            svk = [("sv", ti, wp) for wp in range(8)]
            mc, cc = scol(), scol()
            p.emit("vector", red(mc, sv_t(ti), ALU.max, absval=True), rd=svk,
                   wr=[("mcs",)])
            p.emit("vector", ts(mc, mc, 1e-5, None, ALU.max), rd=[("mcs",)],
                   wr=[("mcs",)])
            p.emit("vector", ts(inv_ssv[:, tc:tc + 1], mc, 1.0 / 127.0),
                   rd=[("mcs",)], wr=[("issv", tc)])
            p.emit("vector", recip(cc, mc), rd=[("mcs",)], wr=[("ccs",)])
            p.emit("vector", ts(cc, cc, 127.0), rd=[("ccs",)], wr=[("ccs",)])
            p.emit("vector", ts(sv_t(ti), sv_t(ti), cc, MAGIC, ALU.mult,
                                ALU.add), rd=[("ccs",)] + svk, wr=svk)
            p.emit("vector", ts(sv_t(ti), sv_t(ti), MAGIC, None, ALU.subtract),
                   rd=svk, wr=svk)
            p.emit("vector", cp(svq_i, sv_t(ti)), rd=svk, wr=[("svi",)])
            p.emit("scalar", act(svq_b, svq_i, AF.Copy), rd=[("svi",)],
                   wr=[("svb",)])
            sv_view = PV(96, 128, 32768, BF16).rearrange(
                "p (k t c) -> p k t c", k=32, t=8)
            for k0 in (0, 8, 16, 24):
                transpose_group(
                    [(svq_b[:, k * 128:(k + 1) * 128], [("svb",)])
                     for k in range(k0, k0 + 8)],
                    sv_view[:, k0:k0 + 8, tc, :],
                    [("svqT", k, tc) for k in range(k0, k0 + 8)])

    TAPS["svqT"] = (PV(96, 128, 32768, BF16), 128)
    if upto <= 6:
        return _close(nc, p, TAPS, tap_names)

    p.fence()
    def xo1_t(tc):
        return PV(32 + tc * 8, 128, D, F32)
    def zT_t(k, tc):
        return PV((k * 8 + tc) * 0.25, 128, 128, BF16)
    wfsd = [PV(160, 128, 2048, F32), PV(168, 128, 2048, F32)]
    wqhd = [PV(176, 128, 2048, BF16), PV(180, 128, 2048, BF16)]
    qa_d = PV(184, 128, 2048, BF16)
    qb_d = PV(188, 128, 2048, BF16)
    x2rd = [PV(192, 128, 512, F32), PV(192, 128, 512, F32)]
    a_dn = scol(8)
    for tc in range(NTC):
        p.emit("vector", ts(a_dn[:, tc:tc + 1], inv_ssv[:, tc:tc + 1],
                            inv_sw["dn"]),
               rd=[("issv", tc), ("isw", "dn")], wr=[("adn", tc)])

    flatdn = wdnT.rearrange("(c p) f -> p c f", p=128)
    for w in range(4):
        for g in range(2):
            tcs = list(range(g * 4, g * 4 + 4))
            for quarter in range(8):
                b = quarter % 2
                bkey = ("wbd", b)
                if g == 0:
                    p.dma("sync", wfsd[b].rearrange("p (k f) -> p k f", k=4),
                          flatdn[:, quarter * 4:quarter * 4 + 4,
                                 w * 512:(w + 1) * 512],
                          wr=[bkey], tag="w_dn")
                    p.emit("vector", ts(qa_d, wfsd[b][:, 0:2048], sw_pp["dn"],
                                        0.5, ALU.mult, ALU.is_gt),
                           rd=[bkey, ("sw", "dn")], wr=[("qa",)])
                    p.emit("vector", ts(qb_d, wfsd[b][:, 0:2048], sw_pp["dn"],
                                        -0.5, ALU.mult, ALU.is_lt),
                           rd=[bkey, ("sw", "dn")], wr=[("qb",)])
                    p.emit("vector", tt(wqhd[b][:, 0:2048], qa_d, qb_d,
                                        ALU.subtract), rd=[("qa",), ("qb",)],
                           wr=[(bkey, kk) for kk in range(4)] + [("wqd", b)])
                    p.dma("sync", wdnq_hbm[w, quarter], wqhd[b],
                          rd=[(bkey, kk) for kk in range(4)] + [("wqd", b)],
                          wr=[("wdq", w, quarter)], tag="w_dnq_out")
                else:
                    p.dma("sync", wqhd[b], wdnq_hbm[w, quarter],
                          rd=[("wdq", w, quarter)],
                          wr=[(bkey, kk) for kk in range(4)] + [("wqd", b)],
                          tag="w_dnq_in")
                for ti, tc in enumerate(tcs):
                    bank = ti
                    for kk in range(4):
                        k = quarter * 4 + kk
                        p.emit("tensor", mmf(pm[bank][:], svqT_t(k, tc),
                                             wqhd[b][:, kk * 512:(kk + 1) * 512],
                                             k == 0, k == DN_K - 1),
                               rd=[("svqT", k, tc), (bkey, kk)],
                               wr=[("pm", bank)], tag="mm_dn", nsw=(k != 0))
                    if quarter == 7:
                        dst = xo1_t(tc)[:, w * 512:(w + 1) * 512]
                        p.dma("sync", x2rd[0],
                              x2_hbm[tc * 128:(tc + 1) * 128,
                                     w * 512:(w + 1) * 512],
                              rd=[("x2h", tc)], wr=[("x2r", 0)], tag="x2_rd")
                        p.emit("vector", ts(dst, pm[bank][:],
                                            a_dn[:, tc:tc + 1]),
                               rd=[("pm", bank), ("adn", tc)],
                               wr=[("xo1", tc, w)])
                        p.emit("vector", tt(dst, dst, x2rd[0], ALU.add),
                               rd=[("xo1", tc, w), ("x2r", 0)],
                               wr=[("xo1", tc, w)])
    p.fence()
    zb16 = PV(160, 128, D, BF16)
    sqs5 = PV(164, 128, D, BF16)
    zscr = PV(168, 128, D, F32)
    for tc in range(NTC):
        xok = [("xo1", tc, w) for w in range(4)]
        al3, nb3, _, m3c = ln_stats(tc, xo1_t(tc), xok, "l3", sqs5, ("sqs5",))
        p.emit("vector", ts(zscr, xo1_t(tc), m3c, None, ALU.subtract),
               rd=xok + [("l3m", tc)], wr=[("zsc",)])
        p.emit("vector", ts(zb16, zscr, al3), rd=[("zsc",), ("l3a", tc)],
               wr=[("zb",)])
        z_view = PV(0, 128, 16384, BF16).rearrange(
            "p (k t c) -> p k t c", k=16, t=8)
        for k0 in (0, 8):
            transpose_group(
                [(zb16[:, k * 128:(k + 1) * 128], [("zb",)])
                 for k in range(k0, k0 + 8)],
                z_view[:, k0:k0 + 8, tc, :],
                [("zT", k, tc) for k in range(k0, k0 + 8)])
        p.dma("sync", xo1_hbm[tc * 128:(tc + 1) * 128, :], xo1_t(tc), rd=xok,
              wr=[("xoh", tc)], tag="xo1_out")

    TAPS["xo1"] = (PV(32, 128, 16384, F32), 128)
    TAPS["zT"] = (PV(0, 128, 16384, BF16), 128)
    if upto <= 7:
        return _close(nc, p, TAPS, tap_names)

    p.fence()
    def h2_t(kh):
        if kh < 32:
            return PV(32 + kh * 2, 128, T, BF16)
        return PV(128 + (kh - 32) * 2, 128, T, BF16)
    wslab = PV(96, 128, 512, BF16)
    wabs = PV(97, 128, 512, BF16)
    shexp = PV(98, 128, 512, BF16)
    mind = PV(99, 128, 512, BF16)
    wsign = PV(100, 128, 512, U16)
    s_row = PV(101, 2, 512, U16)
    sh_row = PV(102, 2, 512, BF16)
    wslab2 = PV(192, 128, 512, BF16)
    wabs2 = PV(193, 128, 512, BF16)
    wq512 = [PV(103 + i, 128, 512, U16) for i in range(16)]
    scrGg = PV(119, 128, 4096, BF16)
    h2sq = PV(127, 128, 512, BF16)
    ssrow = PV(96, 1, T, F32)
    rr3r = PV(100, 1, T, F32)
    rr3c = scol(8)

    _qb = [0]
    _tscr = [(wslab, wabs), (wslab2, wabs2)]

    def ternary_slab(flat_ap, k, c0, dst_ap, dkey, width=512):
        i = _qb[0] % 2
        _qb[0] += 1
        qbank = pq if i == 0 else pm[4]
        qk = ("pq",) if i == 0 else ("pm", 4)
        wsl, wab = _tscr[i]
        she, mnd, wsg, srow_, shrow_ = shexp, mind, wsign, s_row, sh_row
        S = str(i)
        p.dma("gpsimd", wsl[:, 0:width], flat_ap[:, k, c0:c0 + width],
              wr=[("wsl" + S,)], tag="w_tern")
        p.emit("scalar", act(wab[:, 0:width], wsl[:, 0:width], AF.Abs),
               rd=[("wsl" + S,)], wr=[("wab" + S,)])
        p.emit("tensor", mmf(qbank[0:2, 0:width], gsel, wab[:, 0:width],
                             True, True),
               rd=[("wab" + S,), ("const",)], wr=[qk])
        p.emit("vector", ts(srow_.bitcast(BF16)[:, 0:width], qbank[0:2, 0:width],
                            1.0 / 64.0, 1e-8, ALU.mult, ALU.max),
               rd=[qk], wr=[("sro",)])
        p.emit("vector", ts(shrow_[:, 0:width], srow_.bitcast(BF16)[:, 0:width],
                            0.5), rd=[("sro",)], wr=[("shr",)])
        p.emit("tensor", mmf(qbank[:, 0:width], gexp, shrow_[:, 0:width],
                             True, True),
               rd=[("shr",), ("const",), qk], wr=[qk])
        p.emit("vector", cp(she[:, 0:width], qbank[:, 0:width]), rd=[qk],
               wr=[("she",)])
        p.emit("vector", tt(mnd[:, 0:width], wab[:, 0:width], she[:, 0:width],
                            ALU.is_gt), rd=[("wab" + S,), ("she",)],
               wr=[("mnd",)])
        p.emit("vector", tt(mnd[:, 0:width], mnd[:, 0:width], she[:, 0:width],
                            ALU.mult), rd=[("mnd",), ("she",)],
               wr=[("mnd",)])
        p.emit("vector", ts(mnd[:, 0:width], mnd[:, 0:width], 2.0),
               rd=[("mnd",)], wr=[("mnd",)])
        p.emit("vector", ts(wsg[:, 0:width], wsl.bitcast(U16)[:, 0:width],
                            0x8000, None, ALU.bitwise_and), rd=[("wsl" + S,)],
               wr=[("wsg",)])
        return p.emit("vector", tt(dst_ap[:, 0:width],
                                   mnd.bitcast(U16)[:, 0:width],
                                   wsg[:, 0:width], ALU.bitwise_or),
                      rd=[("mnd",), ("wsg",)], wr=[dkey])

    flatgu = wguT.rearrange("(c p) f -> p c f", p=128)
    for hgrp in range(16):
        for gv in range(2):
            cbase = (gv * 64 + hgrp * 4) * 128
            for k in range(KD):
                ternary_slab(flatgu, k, cbase, wq512[k], ("wq", k))
            for hh in range(4):
                hc = hgrp * 4 + hh
                for th in range(2):
                    bank = th
                    for k in range(KD):
                        zmov = PV((k * 8 + th * 4) * 0.25, 128, 512, BF16)
                        p.emit("tensor",
                               mmf(pm[bank][:],
                                   wq512[k].bitcast(BF16)[:, hh * 128:(hh + 1) * 128],
                                   zmov, k == 0, k == KD - 1),
                               rd=[("wq", k)] + [("zT", k, th * 4 + q)
                                                 for q in range(4)],
                               wr=[("pm", bank)], tag="mm_gu", nsw=(k != 0))
                if gv == 0:
                    for th in range(2):
                        p.emit("scalar",
                               act(scrGg[:, (hh * 2 + th) * 512:
                                         (hh * 2 + th + 1) * 512],
                                   pm[th][:], AF.Silu),
                               rd=[("pm", th)], wr=[("sgg", hh, th)])
                else:
                    for th in range(2):
                        p.emit("vector",
                               tt(h2_t(hc)[:, th * 512:(th + 1) * 512],
                                  scrGg[:, (hh * 2 + th) * 512:
                                        (hh * 2 + th + 1) * 512],
                                  pm[th][:], ALU.mult),
                               rd=[("sgg", hh, th), ("pm", th)],
                               wr=[("h2", hc, th)])
                        p.emit("scalar", act(h2sq,
                                             h2_t(hc)[:, th * 512:(th + 1) * 512],
                                             AF.Square), rd=[("h2", hc, th)],
                               wr=[("h2s",)])
                        p.emit("tensor", mmf(pm[2 + th][0:1, 0:512], onesb, h2sq,
                                             hc == 0, hc == HFF_K - 1),
                               rd=[("h2s",), ("const",)], wr=[("ssp", th)],
                               tag="mm_ss", nsw=(hc != 0))
    p.emit("vector", cp(ssrow[:, 0:512], pm[2][0:1, 0:512]), rd=[("ssp", 0)],
           wr=[("ssr",)])
    p.emit("vector", cp(ssrow[:, 512:1024], pm[3][0:1, 0:512]), rd=[("ssp", 1)],
           wr=[("ssr",)])
    p.emit("vector", ts(rr3r, ssrow, 1.0 / HFF), rd=[("ssr",)], wr=[("r3r",)])
    p.emit("scalar", act(rr3r, rr3r, AF.Ln, bias=cEPSF[0:1]), rd=[("r3r",)],
           wr=[("r3r",)])
    p.emit("scalar", act(rr3r, rr3r, AF.Exp, scale=-0.5), rd=[("r3r",)],
           wr=[("r3r",)])
    for tc in range(NTC):
        p.emit("tensor", trf(pq[:, 8 + tc:9 + tc],
                             rr3r[:, tc * 128:(tc + 1) * 128], identf[0:1, 0:1]),
               rd=[("r3r",), ("const",)] + ([("pq",)] if tc == 0 else []),
               wr=[("pq",)], nsw=(tc != 0))
    p.emit("vector", cp(rr3c[:, 0:8], pq[:, 8:16]), rd=[("pq",)],
           wr=[("rr3", tc) for tc in range(NTC)])

    if upto <= 8:
        return _close(nc, p, TAPS, tap_names)

    p.fence()
    prsl = [PV(103, 128, 512, U16), PV(104, 128, 512, U16)]
    xo1rd = [PV(105, 128, 512, F32), PV(107, 128, 512, F32)]
    outsc = PV(109, 128, 512, F32)
    flatpr = wprT.rearrange("(c p) f -> p c f", p=128)
    for w in range(4):
        for g in range(2):
            tcs = list(range(g * 4, g * 4 + 4))
            for k in range(HFF_K):
                sb = k % 2
                if g == 0:
                    ternary_slab(flatpr, k, w * 512, prsl[sb], ("wpr", sb))
                    p.dma("sync", wprq_hbm[w, k], prsl[sb].bitcast(BF16),
                          rd=[("wpr", sb)], wr=[("wpq", w, k)], tag="w_prq_out")
                else:
                    p.dma("sync", prsl[sb].bitcast(BF16), wprq_hbm[w, k],
                          rd=[("wpq", w, k)], wr=[("wpr", sb)], tag="w_prq_in")
                for ti, tc in enumerate(tcs):
                    p.emit("tensor", mmf(pm[ti][:],
                                         h2_t(k)[:, tc * 128:(tc + 1) * 128],
                                         prsl[sb].bitcast(BF16),
                                         k == 0, k == HFF_K - 1),
                           rd=[("wpr", sb), ("h2", k, tc // 4)],
                           wr=[("pm", ti)], tag="mm_pr", nsw=(k != 0))
            for ti, tc in enumerate(tcs):
                xb = ti % 2
                p.dma("sync", xo1rd[xb],
                      xo1_hbm[tc * 128:(tc + 1) * 128, w * 512:(w + 1) * 512],
                      rd=[("xoh", tc)], wr=[("xor", xb)], tag="xo1_rd")
                p.emit("vector", ts(outsc, pm[ti][:], rr3c[:, tc:tc + 1]),
                       rd=[("pm", ti), ("rr3", tc)], wr=[("ousc",)])
                p.emit("vector", tt(outsc, outsc, xo1rd[xb], ALU.add),
                       rd=[("ousc",), ("xor", xb)], wr=[("ousc",)])
                p.dma("sync", out_hbm[tc * 128:(tc + 1) * 128,
                                      w * 512:(w + 1) * 512],
                      outsc, rd=[("ousc",)], wr=[("outd", tc, w)], tag="out")

    return _close(nc, p, TAPS, tap_names)


def _close(nc, p, TAPS, tap_names):
    p.fence()
    for name in tap_names:
        src = TAPS.get(name)
        if src is None:
            continue
        ap_, rows = src
        dreg = nc.dram_tensor("dbg_" + name, [rows, ap_.shape[-1]], ap_.dtype,
                              kind="ExternalOutput").ap()
        p.dma("sync", dreg, ap_, tag="dbg")

    from contextlib import ExitStack
    stack = ExitStack()
    sems = {sn: stack.enter_context(nc.semaphore("sem_" + sn))
            for sn in p.semnames}
    block = stack.enter_context(nc.Block())
    final = [(sn, p.count[sn]) for sn in p.semnames if p.count[sn] > 0]

    def make(engname):
        def body(engobj):
            water = {}
            for fn, deps, own_sem, inc, tag in p.q[engname]:
                need = {}
                for (de, dv) in deps:
                    if dv > water.get(de, 0):
                        need[de] = max(need.get(de, 0), dv)
                for de, dv in need.items():
                    engobj.wait_ge(sems[de], dv)
                    water[de] = dv
                instr = fn(engobj)
                instr.then_inc(sems[own_sem], inc)
            if engname == "sync":
                for (de, dv) in final:
                    engobj.wait_ge(sems[de], dv)
        return body

    block.sync(make("sync"))
    block.vector(make("vector"))
    block.scalar(make("scalar"))
    block.tensor(make("tensor"))
    block.gpsimd(make("gpsimd"))
    stack.close()
    return nc, p


def _rope_tables(core):
    inv_freq = (1.0 / (np.float32(10000.0) **
                       (np.arange(0, HD, 2, dtype=np.float32) / np.float32(HD))
                       )).astype(np.float32)
    pos = (1024 * (core % 2) + np.arange(T)).astype(np.float32)
    freqs = np.outer(pos, inv_freq).astype(np.float32)
    return (np.cos(freqs).astype(np.float32), np.sin(freqs).astype(np.float32))


def _mask0():
    m = np.zeros((128, 128), np.float32)
    r = np.arange(128)
    m[r[:, None] < r[None, :]] = -3.0e38
    return m


def _mask1():
    m = np.zeros((128, 256), np.float32)
    r = np.arange(128)
    c = np.arange(128)
    sub = np.zeros((128, 128), np.float32)
    sub[r[:, None] < c[None, :]] = -3.0e38
    m[:, 128:] = sub
    return m


def _gsel():
    g = np.zeros((128, 2), BF)
    g[0:64, 0] = 1
    g[64:128, 1] = 1
    return g


def _gexp():
    g = np.zeros((2, 128), BF)
    g[0, 0:64] = 1
    g[1, 64:128] = 1
    return g


def host_prepare(inputs):
    x = np.ascontiguousarray(np.asarray(inputs["L0_attn_x"], dtype=np.float32)
                             .reshape(8 * T, D))
    f32 = np.float32
    shared = {
        "wqkvT": np.ascontiguousarray(np.asarray(inputs["L0_attn_qkv_w"], f32).T),
        "woT": np.ascontiguousarray(np.asarray(inputs["L0_attn_o_w"], f32).T),
        "wupT": np.ascontiguousarray(np.asarray(inputs["L0_attn_ffn_up_w"], f32).T),
        "wdnT": np.ascontiguousarray(np.asarray(inputs["L0_attn_ffn_down_w"], f32).T),
        "wguT": np.ascontiguousarray(np.asarray(inputs["L0_ffn_w_gate_up"], f32).T),
        "wprT": np.ascontiguousarray(np.asarray(inputs["L0_ffn_w_proj"], f32).T),
        "anw": np.ascontiguousarray(np.broadcast_to(
            np.asarray(inputs["L0_attn_attn_norm_w"], f32), (128, D))),
        "fnw": np.ascontiguousarray(np.broadcast_to(
            np.asarray(inputs["L0_attn_ffn_norm_w"], f32), (128, D))),
        "mask0": _mask0(), "mask1": _mask1(),
        "identb": np.eye(128, dtype=BF), "identf": np.eye(128, dtype=np.float32),
        "gsel": _gsel(), "gexp": _gexp(),
    }
    cores = []
    for c in range(8):
        cos_t, sin_t = _rope_tables(c)
        d = dict(shared)
        d["x"] = np.ascontiguousarray(x[c * T:(c + 1) * T])
        d["cos_t"] = cos_t
        d["sin_t"] = sin_t
        cores.append(d)
    return cores


def kernel(**inputs):
    cores = host_prepare(inputs)
    nc, p = build_program()
    res = run_bass_kernel_spmd(nc, cores, core_ids=list(range(8)))
    out = np.concatenate([r["out"] for r in res.results], axis=0)
    return np.ascontiguousarray(out.reshape(4, 2048, 2048))
